# version 13
# baseline (speedup 1.0000x reference)
import contextlib
import math
import sys

import numpy as np

try:
    import concourse.bass as bass
except ImportError:
    sys.path.insert(0, "/opt/trn_rl_repo")
    import concourse.bass as bass

import ml_dtypes
import concourse.mybir as mybir
from concourse.bass_utils import run_bass_kernel_spmd
from concourse.aot_env import allow_ici_all_to_all

F32 = mybir.dt.float32
BF16 = mybir.dt.bfloat16
AF = mybir.ActivationFunctionType
ALU = mybir.AluOpType
AX = mybir.AxisListType
BF = ml_dtypes.bfloat16

W = 8
S = 8192
D = 4096
HD = 128
NT = 1024
HL = 4
P = 128
MAGIC = 12582912.0
EXPC = -26.0
SSCL = 1.0 / math.sqrt(128.0)
MASKV = -1.0e9
VW = 132
GEPS = 1e-30

GORDER = [0, 2, 4, 6, 8, 10, 12, 14, 1, 3, 5, 7, 9, 11, 13, 15]

_CHAIN_OPS = frozenset((
    "tensor_tensor", "tensor_add", "tensor_sub", "tensor_mul", "tensor_max",
    "tensor_scalar", "tensor_single_scalar", "tensor_scalar_add",
    "tensor_scalar_sub", "tensor_scalar_mul", "tensor_scalar_max",
    "tensor_scalar_min", "tensor_copy", "tensor_reduce", "reduce_sum",
    "reduce_max", "scalar_tensor_tensor", "memset", "memzero", "reciprocal",
    "activation", "copy", "mul", "add", "sqrt", "square", "select",
))
_RECORD_OPS = frozenset(("matmul", "transpose", "dma_start",
                         "collective_compute"))


class Eng:

    def __init__(self, eng, st):
        object.__setattr__(self, "_eng", eng)
        object.__setattr__(self, "_st", st)

    def __getattr__(self, name):
        target = getattr(self._eng, name)
        if name in _CHAIN_OPS:
            def wrap(*a, **k):
                st = self._st
                if st["last"] is not None:
                    if st["sync"] is None:
                        st["n"] += 1
                        st["last"].then_inc(st["h"], 1)
                        st["sync"] = (st["h"], st["n"])
                    self._eng.wait_ge(*st["sync"])
                ins = target(*a, **k)
                st["last"] = ins
                st["sync"] = None
                return ins
            return wrap
        if name in _RECORD_OPS:
            def wrap2(*a, **k):
                ins = target(*a, **k)
                self._st["last"] = ins
                self._st["sync"] = None
                return ins
            return wrap2
        return target


class Sem:
    __slots__ = ("h", "n")

    def __init__(self, h):
        self.h = h
        self.n = 0

    def tag(self, eng, k=1, to=None):
        st = eng._st
        st["last"].then_inc(self.h, k)
        self.n += k
        st["sync"] = (self.h, to if to is not None else self.n)
        return self.n


def _attn_sched():
    blocks = []
    loads = []
    for gi, g in enumerate(GORDER):
        for s in range(g + 1):
            loads.append(dict(gi=gi, g=g, s=s, li=len(loads),
                              lastblk=len(blocks) + 3))
            for h in range(HL):
                diag = (s == g)
                avs = [list(range(4)) if not diag else list(range(qt + 1))
                       for qt in range(4)]
                blocks.append(dict(b=len(blocks), gi=gi, g=g, s=s, h=h,
                                   diag=diag, avs=avs, li=len(loads) - 1))
    cummask = []
    cnt = 0
    for blk in blocks:
        for u in (0, 1):
            if blk["diag"]:
                cnt += 2
            cummask.append(cnt)
    ginfo = []
    for gi, g in enumerate(GORDER):
        blks = [b for b in blocks if b["gi"] == gi]
        ginfo.append(dict(gi=gi, g=g, first=blks[0]["b"], last=blks[-1]["b"]))
    return blocks, loads, cummask, ginfo


def build_nc():
    nc = bass.Bass("TRN2", target_bir_lowering=False, debug=False)

    hid = nc.dram_tensor("hid", [NT, D], F32, kind="ExternalInput").ap()
    wkS = nc.dram_tensor("wkS", [512, D], F32, kind="ExternalInput").ap()
    wqS = nc.dram_tensor("wqS", [512, D], F32, kind="ExternalInput").ap()
    wvS = nc.dram_tensor("wvS", [512, D], F32, kind="ExternalInput").ap()
    woS = nc.dram_tensor("woS", [512, D], F32, kind="ExternalInput").ap()
    worgT = nc.dram_tensor("worgT", [D, D], F32, kind="ExternalInput").ap()
    ldh = nc.dram_tensor("ldh", [P, 32, 512], F32, kind="ExternalInput").ap()
    uh = nc.dram_tensor("uh", [P, 4, D], F32, kind="ExternalInput").ap()
    wqfT = nc.dram_tensor("wqfT", [P, 32, 128], F32, kind="ExternalInput").ap()
    keysT = nc.dram_tensor("keysT", [P, 8], F32, kind="ExternalInput").ap()
    cctab = nc.dram_tensor("cctab", [P, NT], F32, kind="ExternalInput").ap()
    sstab = nc.dram_tensor("sstab", [P, NT], F32, kind="ExternalInput").ap()
    tric = nc.dram_tensor("tric", [P, P], F32, kind="ExternalInput").ap()
    mqvc = nc.dram_tensor("mqvc", [P, P], F32, kind="ExternalInput").ap()
    imqvc = nc.dram_tensor("imqvc", [P, P], F32, kind="ExternalInput").ap()
    identc = nc.dram_tensor("identc", [P, P], BF16, kind="ExternalInput").ap()
    selEc = nc.dram_tensor("selEc", [8, 512], F32, kind="ExternalInput").ap()
    ccolc = nc.dram_tensor("ccolc", [P, 8], F32, kind="ExternalInput").ap()
    out = nc.dram_tensor("out", [NT, D], F32, kind="ExternalOutput").ap()

    wc = {n: nc.dram_tensor(f"wc_{n}", [D, 512], BF16).ap() for n in "kqvo"}
    wg = {n: nc.dram_tensor(f"wg_{n}", [W * D, 512], BF16).ap() for n in "kqvo"}
    kcon = nc.dram_tensor("kcon", [D, NT], BF16).ap()
    qcon = nc.dram_tensor("qcon", [D, NT], BF16).ap()
    vcon = nc.dram_tensor("vcon", [S, HL * VW], BF16).ap()
    kgd = nc.dram_tensor("kgd", [D, NT], BF16).ap()
    qgd = nc.dram_tensor("qgd", [D, NT], BF16).ap()
    vgd = nc.dram_tensor("vgd", [S, HL * VW], BF16).ap()
    oconA = nc.dram_tensor("oconA", [W * 512, 512], BF16).ap()
    oconB = nc.dram_tensor("oconB", [W * 512, 512], BF16).ap()
    ogA = nc.dram_tensor("ogA", [W * 512, 512], BF16).ap()
    ogB = nc.dram_tensor("ogB", [W * 512, 512], BF16).ap()
    x1raw_d = nc.dram_tensor("x1raw_d", [NT, D], F32).ap()
    x1_d = nc.dram_tensor("x1_d", [NT, D], F32).ap()
    x2raw_d = nc.dram_tensor("x2raw_d", [NT, D], F32).ap()
    mc_in = nc.dram_tensor("mc_in", [P, 32], F32).ap()
    mc_out = nc.dram_tensor("mc_out", [P, 32], F32).ap()

    RG = [list(range(W))]

    with contextlib.ExitStack() as ST:
        e = ST.enter_context
        ccol = e(nc.sbuf_tensor("ccol", [P, 8], F32))
        tri = e(nc.sbuf_tensor("tri", [P, P], F32))
        mqv = e(nc.sbuf_tensor("mqv", [P, P], F32))
        imqv = e(nc.sbuf_tensor("imqv", [P, P], F32))
        ident = e(nc.sbuf_tensor("ident", [P, P], BF16))
        stA = e(nc.sbuf_tensor("stA", [P, 32], F32))
        prtS = e(nc.sbuf_tensor("prtS", [P, 2], F32))
        prtQ = e(nc.sbuf_tensor("prtQ", [P, 2], F32))
        dmj = e(nc.sbuf_tensor("dmj", [P, 512], BF16))
        sqj = e(nc.sbuf_tensor("sqj", [P, 512], BF16))
        sSQ3 = Sem(e(nc.semaphore("sSQ3")))
        sST3 = Sem(e(nc.semaphore("sST3")))

        sB = Sem(e(nc.semaphore("sB")))
        sAG = Sem(e(nc.semaphore("sAG")))
        sLC = Sem(e(nc.semaphore("sLC")))
        sWS = Sem(e(nc.semaphore("sWS")))
        chain = {}
        for en in ("sy", "te", "sc", "ve", "gp"):
            chain[en] = {"h": e(nc.semaphore(f"chain_{en}")), "n": 0,
                         "last": None, "sync": None}

        block = e(nc.Block())
        phase_no = [0]

        def sec(emitter, en):
            def reg(f):
                emitter(lambda eng: f(Eng(eng, chain[en])))
            return reg

        def do_barrier(skip_gp=False):
            phase_no[0] += 1
            tgt = 5 * phase_no[0]
            emits = [block.sync, block.tensor, block.scalar, block.vector]
            if not skip_gp:
                emits.append(block.gpsimd)
            for emit in emits:
                def _f(eng, _t=tgt):
                    eng.nop().then_inc(sB.h, 1)
                    eng.wait_ge(sB.h, _t)
                emit(_f)
            sB.n = tgt

        def quant_emit(ve, x, qout, scr, gmax, ginv, gs, FD,
                       mq=None, imq=None, mqc=None, imqc=None):
            G = FD // 32
            xg = x.rearrange("p (g k) -> p g k", k=32)
            ve.tensor_reduce(gmax, xg, AX.X, ALU.max, apply_absolute_value=True)
            ve.tensor_scalar(ginv, gmax, GEPS, None, ALU.max)
            ve.reciprocal(ginv, ginv)
            if mq is not None:
                ve.tensor_tensor(ginv, ginv, mq, ALU.mult)
                ve.tensor_tensor(gs, gmax, imq, ALU.mult)
            else:
                ve.tensor_scalar(ginv, ginv, mqc, None, ALU.mult)
                ve.tensor_scalar(gs, gmax, imqc, None, ALU.mult)
            gb = ginv.rearrange("p (g o) -> p g o", o=1).broadcast_to([P, G, 32])
            sb_ = gs.rearrange("p (g o) -> p g o", o=1).broadcast_to([P, G, 32])
            scrg = scr.rearrange("p (g k) -> p g k", k=32)
            ve.tensor_tensor(scrg, xg, gb, ALU.mult)
            ve.tensor_scalar(scr, scr, MAGIC, -MAGIC, ALU.add, ALU.add)
            ve.tensor_tensor(qout.rearrange("p (g k) -> p g k", k=32),
                             scrg, sb_, ALU.mult)

        SP01 = ST.enter_context(contextlib.ExitStack())
        qhid_fm = SP01.enter_context(
            nc.sbuf_tensor("qhid_fm", [P, 32 * NT], BF16))
        sTE = Sem(SP01.enter_context(nc.semaphore("sTE")))

        with contextlib.ExitStack() as S0:
            e0 = S0.enter_context
            wtile = [e0(nc.sbuf_tensor(f"wtile{i}", [P, D], F32)) for i in (0, 1)]
            wq16 = [e0(nc.sbuf_tensor(f"wq16{i}", [P, D], BF16))
                    for i in (0, 1)]
            scr0 = e0(nc.sbuf_tensor("scr0", [P, D], F32))
            gmax0 = e0(nc.sbuf_tensor("gmax0", [P, P], F32))
            ginv0 = e0(nc.sbuf_tensor("ginv0", [P, P], F32))
            gs0 = e0(nc.sbuf_tensor("gs0", [P, P], F32))
            wstg = e0(nc.sbuf_tensor("wstg", [P, 32, 512], BF16))
            hq16 = [e0(nc.sbuf_tensor(f"hq16{i}", [P, D], BF16)) for i in (0, 1)]
            tps = [e0(nc.psum_tensor(f"tps{i}", [P, P], BF16)) for i in range(4)]

            sWL = [Sem(e0(nc.semaphore(f"sWL{i}"))) for i in (0, 1)]
            sQZ = Sem(e0(nc.semaphore("sQZ")))
            sTP = Sem(e0(nc.semaphore("sTP")))
            sHL_ = [Sem(e0(nc.semaphore(f"sHL{i}_"))) for i in (0, 1)]
            sHQ = Sem(e0(nc.semaphore("sHQ")))

            worder = "kqvo"
            wsrc = {"k": wkS, "q": wqS, "v": wvS, "o": woS}
            AG_W = {wn: wi + 1 for wi, wn in enumerate(worder)}

            @sec(block.sync, "sy")
            def _(sy):
                for t, src in ((ccol, ccolc), (tri, tric), (mqv, mqvc),
                               (imqv, imqvc), (ident, identc)):
                    sy.dma_start(out=t[:], in_=src)
                    sLC.tag(sy, 16)
                for wi, wn in enumerate(worder):
                    for rt in range(4):
                        i = wi * 4 + rt
                        if i >= 2:
                            sy.wait_ge(sQZ.h, i - 1)
                        sy.dma_start(out=wtile[i % 2][:],
                                     in_=wsrc[wn][rt * P:(rt + 1) * P, :])
                        sWL[i % 2].tag(sy, 16)
                    sy.wait_ge(sTE.h, (wi + 1) * 128)
                    sy.dma_start(
                        out=wc[wn].rearrange("(fc p) r -> p fc r", p=P),
                        in_=wstg[:])
                    sWS.tag(sy, 16)
                for t in range(8):
                    if t < 2:
                        sy.wait_ge(sQZ.h, 16)
                    else:
                        sy.wait_ge(sHQ.h, t - 1)
                    sy.dma_start(out=wtile[t % 2][:],
                                 in_=hid[t * P:(t + 1) * P, :])
                    sHL_[t % 2].tag(sy, 16)
                sy.wait_ge(sWS.h, 16 * 4)

            @sec(block.vector, "ve")
            def _(ve):
                ve.wait_ge(sLC.h, 16 * 5)
                for wi, wn in enumerate(worder):
                    for rt in range(4):
                        i = wi * 4 + rt
                        ve.wait_ge(sWL[i % 2].h, 16 * (i // 2 + 1))
                        if i > 1:
                            ve.wait_ge(sTP.h, (i - 1) * 32)
                        quant_emit(ve, wtile[i % 2][:], wq16[i % 2][:], scr0[:],
                                   gmax0[:], ginv0[:], gs0[:], D,
                                   mq=mqv[:], imq=imqv[:])
                        sQZ.tag(ve)
                for t in range(8):
                    ve.wait_ge(sHL_[t % 2].h, 16 * (t // 2 + 1))
                    if t >= 2:
                        ve.wait_ge(sTP.h, 512 + (t - 1) * 32)
                    quant_emit(ve, wtile[t % 2][:], hq16[t % 2][:], scr0[:],
                               gmax0[:], ginv0[:], gs0[:], D,
                               mq=mqv[:], imq=imqv[:])
                    sHQ.tag(ve)

            @sec(block.scalar, "sc")
            def _(sc):
                for wi, wn in enumerate(worder):
                    for rt in range(4):
                        i = wi * 4 + rt
                        if rt == 0 and wi > 0:
                            sc.wait_ge(sWS.h, 16 * wi)
                        for fc in range(32):
                            j = i * 32 + fc
                            sc.wait_ge(sTP.h, j + 1)
                            sc.copy(wstg[:, fc, rt * P:(rt + 1) * P],
                                    tps[j % 4][:])
                            sTE.tag(sc)
                for t in range(8):
                    for fc in range(32):
                        j = 512 + t * 32 + fc
                        sc.wait_ge(sTP.h, j + 1)
                        sc.copy(qhid_fm[:, fc * NT + t * P:
                                        fc * NT + (t + 1) * P],
                                tps[j % 4][:])
                        sTE.tag(sc)

            @sec(block.tensor, "te")
            def _(te):
                te.wait_ge(sLC.h, 16 * 5)
                for wi, wn in enumerate(worder):
                    for rt in range(4):
                        i = wi * 4 + rt
                        te.wait_ge(sQZ.h, i + 1)
                        for fc in range(32):
                            j = i * 32 + fc
                            if j >= 4:
                                te.wait_ge(sTE.h, j - 3)
                            te.transpose(tps[j % 4][:],
                                         wq16[i % 2][:, fc * P:(fc + 1) * P],
                                         ident[:])
                            sTP.tag(te)
                for t in range(8):
                    te.wait_ge(sHQ.h, t + 1)
                    for fc in range(32):
                        j = 512 + t * 32 + fc
                        te.wait_ge(sTE.h, j - 3)
                        te.transpose(tps[j % 4][:],
                                     hq16[t % 2][:, fc * P:(fc + 1) * P],
                                     ident[:])
                        sTP.tag(te)

            @sec(block.gpsimd, "gp")
            def _(gp):
                gp.nop().then_inc(sB.h, 1)
                for wi, wn in enumerate("kqv"):
                    gp.wait_ge(sWS.h, 16 * (wi + 1))
                    gp.collective_compute(
                        "AllGather", ALU.bypass, replica_groups=RG,
                        ins=[wc[wn].opt()], outs=[wg[wn].opt()])
                    sAG.tag(gp)

            do_barrier(skip_gp=True)

        with contextlib.ExitStack() as S1:
            e1 = S1.enter_context
            cct = e1(nc.sbuf_tensor("cct", [P, NT], F32))
            sst = e1(nc.sbuf_tensor("sst", [P, NT], F32))
            wblk = [e1(nc.sbuf_tensor(f"wblk{i}", [P, 32, 512], BF16))
                    for i in range(3)]
            ropeA = e1(nc.sbuf_tensor("ropeA", [P, 512], F32))
            ropeB = e1(nc.sbuf_tensor("ropeB", [P, 512], F32))
            ropeS = e1(nc.sbuf_tensor("ropeS", [P, 512], F32))
            kstg = [e1(nc.sbuf_tensor(f"kstg{i}", [P, 512], BF16)) for i in (0, 1)]
            qbst = [e1(nc.sbuf_tensor(f"qbst{i}", [P, 512], BF16)) for i in (0, 1)]
            vstg = [e1(nc.sbuf_tensor(f"vstg{i}", [P, HL * VW], BF16))
                    for i in (0, 1)]
            pps = [e1(nc.psum_tensor(f"pps{i}", [P, 512], F32)) for i in range(3)]

            sCT = Sem(e1(nc.semaphore("sCT")))
            sKWB = [Sem(e1(nc.semaphore(f"sKWB{i}"))) for i in (0, 1, 2)]
            sVWB = [Sem(e1(nc.semaphore(f"sVWB{i}"))) for i in (0, 1, 2)]
            sPK = Sem(e1(nc.semaphore("sPK")))
            sRKv = Sem(e1(nc.semaphore("sRKv")))
            sVE = Sem(e1(nc.semaphore("sVE")))
            sVI = Sem(e1(nc.semaphore("sVI")))
            sCNK = [Sem(e1(nc.semaphore(f"sCNK{i}"))) for i in (0, 1)]
            sCNQ = [Sem(e1(nc.semaphore(f"sCNQ{i}"))) for i in (0, 1)]
            sCNV = [Sem(e1(nc.semaphore(f"sCNV{i}"))) for i in (0, 1)]

            @sec(block.sync, "sy")
            def _(sy):
                sy.dma_start(out=cct[:], in_=cctab)
                sCT.tag(sy, 16)
                sy.dma_start(out=sst[:], in_=sstab)
                sCT.tag(sy, 16)
                for m in range(32):
                    if m % 4 == 0:
                        rb = m // 4
                        if rb >= 3:
                            sy.wait_ge(sPK.h, 8 * (rb - 2))
                        sy.wait_ge(sAG.h, AG_W["k"])
                        sy.dma_start(
                            out=wblk[rb % 3][:],
                            in_=wg["k"][rb * D:(rb + 1) * D, :]
                            .rearrange("(fc p) n -> p fc n", p=P))
                        sKWB[rb % 3].tag(sy, 16)
                    if m >= 1:
                        for tt in range(2):
                            i = (m - 1) * 2 + tt
                            sy.wait_ge(sRKv.h, i + 1)
                            sy.dma_start(
                                out=kcon[(m - 1) * P:m * P,
                                         tt * 512:(tt + 1) * 512],
                                in_=kstg[i % 2][:])
                            sCNK[i % 2].tag(sy, 16)
                for tt in range(2):
                    i = 31 * 2 + tt
                    sy.wait_ge(sRKv.h, i + 1)
                    sy.dma_start(out=kcon[31 * P:32 * P,
                                          tt * 512:(tt + 1) * 512],
                                 in_=kstg[i % 2][:])
                    sCNK[i % 2].tag(sy, 16)
                for m in range(32):
                    if m % 4 == 0:
                        rb = m // 4
                        j = 8 + rb
                        if rb >= 3:
                            sy.wait_ge(sPK.h, 64 + 8 * (rb - 2))
                        else:
                            sy.wait_ge(sPK.h, 64 - 16 + 8 * rb)
                        sy.wait_ge(sAG.h, AG_W["q"])
                        sy.dma_start(
                            out=wblk[j % 3][:],
                            in_=wg["q"][rb * D:(rb + 1) * D, :]
                            .rearrange("(fc p) n -> p fc n", p=P))
                        sKWB[j % 3].tag(sy, 16)
                    if m >= 1:
                        for tt in range(2):
                            qi = (m - 1) * 2 + tt
                            sy.wait_ge(sRKv.h, 64 + qi + 1)
                            sy.dma_start(
                                out=qcon[(m - 1) * P:m * P,
                                         tt * 512:(tt + 1) * 512],
                                in_=qbst[qi % 2][:])
                            sCNQ[qi % 2].tag(sy, 16)
                for tt in range(2):
                    qi = 31 * 2 + tt
                    sy.wait_ge(sRKv.h, 64 + qi + 1)
                    sy.dma_start(out=qcon[31 * P:32 * P,
                                          tt * 512:(tt + 1) * 512],
                                 in_=qbst[qi % 2][:])
                    sCNQ[qi % 2].tag(sy, 16)
                for n in range(8):
                    j = 16 + n
                    if n >= 3:
                        sy.wait_ge(sPK.h, 128 + (n - 2) * 8)
                    else:
                        sy.wait_ge(sPK.h, 128 - 16 + 8 * n)
                    sy.wait_ge(sAG.h, AG_W["v"])
                    sy.dma_start(
                        out=wblk[j % 3][:],
                        in_=wg["v"][n * D:(n + 1) * D, :]
                        .rearrange("(fc p) n -> p fc n", p=P))
                    sVWB[n % 3].tag(sy, 16)
                    if n >= 1:
                        for tt in range(8):
                            vi = (n - 1) * 8 + tt
                            sy.wait_ge(sVI.h, 2)
                            sy.wait_ge(sVE.h, vi + 1)
                            sy.dma_start(
                                out=vcon[(n - 1) * NT + tt * P:
                                         (n - 1) * NT + (tt + 1) * P, :],
                                in_=vstg[vi % 2][:])
                            sCNV[vi % 2].tag(sy, 16)
                for tt in range(8):
                    vi = 7 * 8 + tt
                    sy.wait_ge(sVE.h, vi + 1)
                    sy.dma_start(
                        out=vcon[7 * NT + tt * P: 7 * NT + (tt + 1) * P, :],
                        in_=vstg[vi % 2][:])
                    sCNV[vi % 2].tag(sy, 16)
                sy.wait_ge(sCNK[0].h, 16 * 32)
                sy.wait_ge(sCNK[1].h, 16 * 32)
                sy.wait_ge(sCNQ[0].h, 16 * 32)
                sy.wait_ge(sCNQ[1].h, 16 * 32)
                sy.wait_ge(sCNV[0].h, 16 * 32)
                sy.wait_ge(sCNV[1].h, 16 * 32)

            @sec(block.tensor, "te")
            def _(te):
                te.wait_ge(sTE.h, 768)
                for base, kind in ((0, "k"), (64, "q")):
                    for m in range(32):
                        for tt in range(2):
                            i = base + m * 2 + tt
                            li = m * 2 + tt
                            j = (0 if kind == "k" else 8) + m // 4
                            if tt == 0 and m % 4 == 0:
                                te.wait_ge(sKWB[j % 3].h,
                                           16 * (j // 3 + 1))
                            if kind == "k":
                                if i >= 3:
                                    te.wait_ge(sRKv.h, i - 2)
                            else:
                                te.wait_ge(sRKv.h, 62 + li)
                            for fc in range(32):
                                te.matmul(
                                    pps[i % 3][:],
                                    wblk[j % 3][:, fc,
                                               (m % 4) * P:(m % 4 + 1) * P],
                                    qhid_fm[:, fc * NT + tt * 512:
                                            fc * NT + (tt + 1) * 512],
                                    start=(fc == 0), stop=(fc == 31))
                            sPK.tag(te)
                for n in range(8):
                    j = 16 + n
                    for tt in range(8):
                        vi = n * 8 + tt
                        i = 128 + vi
                        if tt == 0:
                            te.wait_ge(sVWB[n % 3].h, 16 * (n // 3 + 1))
                        if vi < 3:
                            te.wait_ge(sRKv.h, 126 + vi)
                        else:
                            te.wait_ge(sVE.h, vi - 2)
                        for fc in range(32):
                            te.matmul(
                                pps[i % 3][:],
                                qhid_fm[:, fc * NT + tt * P:
                                        fc * NT + (tt + 1) * P],
                                wblk[j % 3][:, fc, :],
                                start=(fc == 0), stop=(fc == 31))
                        sPK.tag(te)

            @sec(block.vector, "ve")
            def _(ve):
                ve.wait_ge(sCT.h, 32)
                for i in range(64):
                    tt = i % 2
                    ve.wait_ge(sPK.h, i + 1)
                    if i >= 2:
                        ve.wait_ge(sCNK[i % 2].h, 16 * (i // 2))
                    c = cct[:, tt * 512:(tt + 1) * 512]
                    s2 = sst[:, tt * 512:(tt + 1) * 512]
                    ps = pps[i % 3][:]
                    ve.tensor_tensor(ropeA[:], ps, c, ALU.mult)
                    ve.tensor_tensor(ropeB[:], ps, s2, ALU.mult)
                    ve.tensor_copy(ropeS[0:64, :], ropeB[64:128, :])
                    ve.tensor_copy(ropeS[64:128, :], ropeB[0:64, :])
                    ve.tensor_tensor(kstg[i % 2][:], ropeA[:], ropeS[:],
                                     ALU.add)
                    sRKv.tag(ve)
                for qi in range(64):
                    tt = qi % 2
                    i = 64 + qi
                    ve.wait_ge(sPK.h, i + 1)
                    if qi >= 2:
                        ve.wait_ge(sCNQ[qi % 2].h, 16 * (qi // 2))
                    c = cct[:, tt * 512:(tt + 1) * 512]
                    s2 = sst[:, tt * 512:(tt + 1) * 512]
                    ps = pps[i % 3][:]
                    ve.tensor_tensor(ropeA[:], ps, c, ALU.mult)
                    ve.tensor_tensor(ropeB[:], ps, s2, ALU.mult)
                    ve.tensor_copy(ropeS[0:64, :], ropeB[64:128, :])
                    ve.tensor_copy(ropeS[64:128, :], ropeB[0:64, :])
                    ve.tensor_tensor(qbst[qi % 2][:], ropeA[:], ropeS[:],
                                     ALU.add)
                    sRKv.tag(ve)
                for bi in (0, 1):
                    ve.memset(vstg[bi][:], 0.0)
                    ve.memset(vstg[bi][:].rearrange("p (h c) -> p h c", c=VW)
                              [:, :, 128:129], 1.0)
                    sVI.tag(ve)
                for n in range(8):
                    for tt in range(8):
                        vi = n * 8 + tt
                        i = 128 + vi
                        ve.wait_ge(sPK.h, i + 1)
                        if vi >= 2:
                            ve.wait_ge(sCNV[vi % 2].h, 16 * (vi // 2))
                        for hl in range(HL):
                            ve.tensor_copy(
                                vstg[vi % 2][:, hl * VW:hl * VW + P],
                                pps[i % 3][:, hl * P:(hl + 1) * P])
                        sVE.tag(ve)

            @sec(block.gpsimd, "gp")
            def _(gp):
                gp.wait_ge(sCNK[0].h, 16 * 32)
                gp.wait_ge(sCNK[1].h, 16 * 32)
                gp.collective_compute(
                    "AllToAll", ALU.bypass, replica_groups=RG,
                    ins=[kcon.opt()], outs=[kgd.opt()])
                sAG.tag(gp)
                gp.wait_ge(sCNQ[0].h, 16 * 32)
                gp.wait_ge(sCNQ[1].h, 16 * 32)
                gp.collective_compute(
                    "AllToAll", ALU.bypass, replica_groups=RG,
                    ins=[qcon.opt()], outs=[qgd.opt()])
                sAG.tag(gp)
                gp.wait_ge(sCNV[0].h, 16 * 32)
                gp.wait_ge(sCNV[1].h, 16 * 32)
                gp.collective_compute(
                    "AllToAll", ALU.bypass, replica_groups=RG,
                    ins=[vcon.opt()], outs=[vgd.opt()])
                sAG.tag(gp)
                gp.wait_ge(sWS.h, 16 * 4)
                gp.collective_compute(
                    "AllGather", ALU.bypass, replica_groups=RG,
                    ins=[wc["o"].opt()], outs=[wg["o"].opt()])
                sAG.tag(gp)

            A2A_V = sAG.n - 1
            AG_W["o"] = sAG.n
            A2A_OA = sAG.n + 1
            do_barrier()

        SP01.close()

        blocks, loads, cummask, ginfo = _attn_sched()

        SP23 = ST.enter_context(contextlib.ExitStack())
        oqfmA = SP23.enter_context(
            nc.sbuf_tensor("oqfmA", [P, 32 * 512], BF16))
        woblk = [SP23.enter_context(
            nc.sbuf_tensor(f"woblk{i}", [P, 32, 512], BF16)) for i in (0, 1)]
        sOFA = Sem(SP23.enter_context(nc.semaphore("sOFA")))
        sWOB = [Sem(SP23.enter_context(nc.semaphore(f"sWOB{i}")))
                for i in (0, 1)]

        with contextlib.ExitStack() as S2:
            e2 = S2.enter_context
            ksg = [e2(nc.sbuf_tensor(f"ksg{i}", [P, HL * 512], BF16))
                   for i in (0, 1)]
            vsg = [e2(nc.sbuf_tensor(f"vsg{i}", [P, 4 * HL * VW], BF16))
                   for i in (0, 1)]
            qsb = [e2(nc.sbuf_tensor(f"qsb{i}", [P, HL * 512], BF16))
                   for i in (0, 1)]
            AC = [e2(nc.sbuf_tensor(f"AC{i}", [P, 2 * HL * 264], F32))
                  for i in (0, 1)]
            expS = [e2(nc.sbuf_tensor(f"expS{i}", [P, 2048], BF16))
                    for i in range(4)]
            rinv = [e2(nc.sbuf_tensor(f"rinv{i}", [P, 16], F32)) for i in (0, 1)]
            onrm = [e2(nc.sbuf_tensor(f"onrm{i}", [P, 512], F32)) for i in (0, 1)]
            oqs = [e2(nc.sbuf_tensor(f"oqs{i}", [P, 512], BF16)) for i in (0, 1)]
            og4 = e2(nc.sbuf_tensor("og4", [P, 16], F32))
            og4i = e2(nc.sbuf_tensor("og4i", [P, 16], F32))
            og4s = e2(nc.sbuf_tensor("og4s", [P, 16], F32))
            oscr = e2(nc.sbuf_tensor("oscr", [P, 512], F32))
            psS = e2(nc.psum_tensor("psS", [P, 2048], F32))
            acc = [[e2(nc.psum_tensor(f"acc{pr}_{pa}", [P, 264], F32))
                    for pa in (0, 1)] for pr in (0, 1)]

            sKS = [Sem(e2(nc.semaphore(f"sKS{i}"))) for i in (0, 1)]
            sQL = [Sem(e2(nc.semaphore(f"sQL{i}"))) for i in (0, 1)]
            sSM = Sem(e2(nc.semaphore("sSM")))
            sMK = Sem(e2(nc.semaphore("sMK")))
            sEX = Sem(e2(nc.semaphore("sEX")))
            sAV = Sem(e2(nc.semaphore("sAV")))
            sACA = Sem(e2(nc.semaphore("sACA")))
            sRV = Sem(e2(nc.semaphore("sRV")))
            sOE = Sem(e2(nc.semaphore("sOE")))
            sQZ2 = Sem(e2(nc.semaphore("sQZ2")))
            sOW = [Sem(e2(nc.semaphore(f"sOW{i}"))) for i in (0, 1)]

            def tok_addr(g512):
                return g512 // 2, (g512 % 2) * 512

            @sec(block.sync, "sy")
            def _(sy):
                ofa_q = [("w", 0), ("w", 1)] + [("a", fc) for fc in range(32)]

                def emit_ofa():
                    kind_, a_ = ofa_q.pop(0)
                    if kind_ == "w":
                        sy.wait_ge(sAG.h, AG_W["o"])
                        sy.dma_start(
                            out=woblk[a_][:],
                            in_=wg["o"][a_ * D:(a_ + 1) * D, :]
                            .rearrange("(fc p) n -> p fc n", p=P))
                        sWOB[a_].tag(sy, 16)
                    else:
                        if a_ == 0:
                            sy.wait_ge(sAG.h, A2A_OA)
                        i2 = a_ // 4
                        f0 = (a_ % 4) * P
                        sy.dma_start(
                            out=oqfmA[:, a_ * 512:(a_ + 1) * 512],
                            in_=ogA[i2 * 512:(i2 + 1) * 512, f0:f0 + P],
                            transpose=True)
                        sOFA.tag(sy, 16)

                sy.wait_ge(sAG.h, A2A_V)
                for gi, g in enumerate(GORDER):
                    for s in range(g + 1):
                        ld = [l for l in loads
                              if l["gi"] == gi and l["s"] == s][0]
                        li = ld["li"]
                        if s == 0:
                            if gi >= 2:
                                sy.wait_ge(sSM.h,
                                           2 * (ginfo[gi - 2]["last"] + 1))
                            i, c0 = tok_addr(g)
                            sy.dma_start(
                                out=qsb[gi % 2][:].rearrange(
                                    "p (h t) -> p h t", t=512),
                                in_=qgd[i * 512:(i + 1) * 512, c0:c0 + 512]
                                .rearrange("(h p) t -> p h t", p=P))
                            sQL[gi % 2].tag(sy, 16)
                        if li >= 2:
                            sy.wait_ge(sAV.h, loads[li - 2]["lastblk"] + 1)
                        i, c0 = tok_addr(s)
                        sy.dma_start(
                            out=ksg[li % 2][:].rearrange(
                                "p (h t) -> p h t", t=512),
                            in_=kgd[i * 512:(i + 1) * 512, c0:c0 + 512]
                            .rearrange("(h p) t -> p h t", p=P))
                        sKS[li % 2].tag(sy, 16)
                        sy.dma_start(
                            out=vsg[li % 2][:].rearrange(
                                "p (a c) -> p a c", c=HL * VW),
                            in_=vgd[s * 512:(s + 1) * 512, :]
                            .rearrange("(a p) c -> p a c", p=P))
                        sKS[li % 2].tag(sy, 16)
                        if s == 1 and gi >= 1:
                            pg = gi - 1
                            gq = GORDER[pg]
                            dst = oconA if (gq % 2 == 0) else oconB
                            j = gq // 2
                            for qt in range(4):
                                oi = pg * 4 + qt
                                sy.wait_ge(sQZ2.h, oi + 1)
                                sy.dma_start(
                                    out=dst[j * 512 + qt * P:
                                            j * 512 + (qt + 1) * P, :],
                                    in_=oqs[oi % 2][:])
                                sOW[oi % 2].tag(sy, 16)
                        if gi >= 11 and ofa_q:
                            emit_ofa()
                gq = GORDER[15]
                dst = oconA if (gq % 2 == 0) else oconB
                j = gq // 2
                for qt in range(4):
                    oi = 15 * 4 + qt
                    sy.wait_ge(sQZ2.h, oi + 1)
                    sy.dma_start(
                        out=dst[j * 512 + qt * P: j * 512 + (qt + 1) * P, :],
                        in_=oqs[oi % 2][:])
                    sOW[oi % 2].tag(sy, 16)
                while ofa_q:
                    emit_ofa()
                sy.wait_ge(sOW[0].h, 16 * 32)
                sy.wait_ge(sOW[1].h, 16 * 32)

            @sec(block.tensor, "te")
            def _(te):
                def emit_avs(blk):
                    b = blk["b"]
                    buf = blk["li"] % 2
                    h = blk["h"]
                    te.wait_ge(sEX.h, 2 * b + 2)
                    if b >= 2:
                        te.wait_ge(sACA.h, b - 1)
                    par = b % 2
                    for qt in range(4):
                        kts = blk["avs"][qt]
                        for ki, kt in enumerate(kts):
                            u, ksl = kt // 2, kt % 2
                            te.matmul(
                                acc[par][qt // 2][:, (qt % 2) * 132:
                                                  (qt % 2) * 132 + 132],
                                expS[b % 4][:, u * 1024 + ksl * 512 + qt * P:
                                            u * 1024 + ksl * 512 +
                                            (qt + 1) * P],
                                vsg[buf][:, kt * (HL * VW) + h * VW:
                                         kt * (HL * VW) + h * VW + 132],
                                start=(ki == 0), stop=(ki == len(kts) - 1))
                    sAV.tag(te)

                pend = []
                for blk in blocks:
                    b, gi, s, h, li = (blk["b"], blk["gi"], blk["s"],
                                       blk["h"], blk["li"])
                    buf = li % 2
                    if h == 0:
                        te.wait_ge(sKS[li % 2].h, 32 * (li // 2 + 1))
                        if s == 0:
                            te.wait_ge(sQL[gi % 2].h, 16 * (gi // 2 + 1))
                    for u in (0, 1):
                        U = 2 * b + u
                        if U >= 2:
                            te.wait_ge(sEX.h, U - 1)
                        for ksl in (0, 1):
                            kt = u * 2 + ksl
                            te.matmul(
                                psS[:, u * 1024 + ksl * 512:
                                    u * 1024 + (ksl + 1) * 512],
                                ksg[buf][:, h * 512 + kt * P:
                                         h * 512 + (kt + 1) * P],
                                qsb[gi % 2][:, h * 512:(h + 1) * 512],
                                start=True, stop=True)
                        sSM.tag(te)
                    pend.append(blk)
                    if len(pend) > 2:
                        emit_avs(pend.pop(0))
                for b_ in pend:
                    emit_avs(b_)

            @sec(block.scalar, "sc")
            def _(sc):
                sc.wait_ge(sLC.h, 16 * 5)

                def emit_evacs(pg, qt):
                    if qt == 0:
                        sc.wait_ge(sRV.h, pg + 1)
                    oi = pg * 4 + qt
                    if oi >= 2:
                        sc.wait_ge(sQZ2.h, oi - 1)
                    for h in range(HL):
                        c0 = (h * 2 + qt // 2) * 264 + (qt % 2) * 132
                        a = (h * 2 + qt // 2) * 2 + (qt % 2)
                        sc.mul(onrm[oi % 2][:, h * P:(h + 1) * P],
                               AC[pg % 2][:, c0:c0 + 128],
                               rinv[pg % 2][:, a:a + 1])
                        sOE.tag(sc)

                for gi, g in enumerate(GORDER):
                    gblocks = [x for x in blocks if x["gi"] == gi]
                    for bi_, blk in enumerate(gblocks):
                        b = blk["b"]
                        for u in (0, 1):
                            U = 2 * b + u
                            sc.wait_ge(sSM.h, U + 1)
                            if cummask[U] > 0:
                                sc.wait_ge(sMK.h, cummask[U])
                            if b >= 4:
                                sc.wait_ge(sAV.h, b - 3)
                            sc.activation(
                                expS[b % 4][:, u * 1024:(u + 1) * 1024],
                                psS[:, u * 1024:(u + 1) * 1024], AF.Exp,
                                bias=ccol[:, 1:2], scale=ccol[:, 0:1])
                            sEX.tag(sc)
                        if bi_ <= 3 and gi >= 1:
                            emit_evacs(gi - 1, bi_)
                for qt in range(4):
                    emit_evacs(15, qt)

            @sec(block.vector, "ve")
            def _(ve):
                def flush_add(pb):
                    b2, s2, h2, gi2 = pb["b"], pb["s"], pb["h"], pb["gi"]
                    ve.wait_ge(sAV.h, b2 + 1)
                    par = b2 % 2
                    for pa in (0, 1):
                        dst = AC[gi2 % 2][:, (h2 * 2 + pa) * 264:
                                          (h2 * 2 + pa + 1) * 264]
                        if s2 == 0:
                            ve.tensor_copy(dst, acc[par][pa][:])
                        else:
                            ve.tensor_tensor(dst, dst, acc[par][pa][:],
                                             ALU.add)
                    sACA.tag(ve)

                def emit_rinv(pg):
                    if pg >= 2:
                        ve.wait_ge(sOE.h, 16 * (pg - 1))
                    rsv = AC[pg % 2][:].rearrange("p (a c) -> p a c", c=132)
                    ve.reciprocal(rinv[pg % 2][:], rsv[:, :, 128:129])
                    sRV.tag(ve)

                pendv = None
                for gi, g in enumerate(GORDER):
                    if gi >= 2:
                        ve.wait_ge(sOE.h, 16 * (gi - 1))
                    gblocks = [x for x in blocks if x["gi"] == gi]
                    for bi_, blk in enumerate(gblocks):
                        b, s, h = blk["b"], blk["s"], blk["h"]
                        if blk["diag"]:
                            for kt in range(4):
                                u, ksl = kt // 2, kt % 2
                                ve.wait_ge(sSM.h, 2 * b + u + 1)
                                ve.tensor_tensor(
                                    psS[:, u * 1024 + ksl * 512 + kt * P:
                                        u * 1024 + ksl * 512 + (kt + 1) * P],
                                    psS[:, u * 1024 + ksl * 512 + kt * P:
                                        u * 1024 + ksl * 512 + (kt + 1) * P],
                                    tri[:], ALU.add)
                                sMK.tag(ve)
                        if pendv is not None:
                            flush_add(pendv)
                            if pendv["gi"] != gi:
                                emit_rinv(pendv["gi"])
                        pendv = blk
                        if bi_ <= 3 and gi >= 1:
                            pg, qt = gi - 1, bi_
                            oi = pg * 4 + qt
                            ve.wait_ge(sOE.h, pg * 16 + (qt + 1) * 4)
                            if oi >= 2:
                                ve.wait_ge(sOW[oi % 2].h, 16 * (oi // 2))
                            quant_emit(ve, onrm[oi % 2][:],
                                       oqs[oi % 2][:], oscr[:], og4[:],
                                       og4i[:], og4s[:], 512,
                                       mqc=ccol[:, 2:3],
                                       imqc=ccol[:, 3:4])
                            sQZ2.tag(ve)
                flush_add(pendv)
                emit_rinv(15)
                for qt in range(4):
                    oi = 15 * 4 + qt
                    ve.wait_ge(sOE.h, 15 * 16 + (qt + 1) * 4)
                    if oi >= 2:
                        ve.wait_ge(sOW[oi % 2].h, 16 * (oi // 2))
                    quant_emit(ve, onrm[oi % 2][:], oqs[oi % 2][:],
                               oscr[:], og4[:], og4i[:], og4s[:], 512,
                               mqc=ccol[:, 2:3], imqc=ccol[:, 3:4])
                    sQZ2.tag(ve)

            @sec(block.gpsimd, "gp")
            def _(gp):
                gp.wait_ge(sOW[0].h, 16 * 16)
                gp.wait_ge(sOW[1].h, 16 * 16)
                gp.collective_compute(
                    "AllToAll", ALU.bypass, replica_groups=RG,
                    ins=[oconA.opt()], outs=[ogA.opt()])
                sAG.tag(gp)
                gp.wait_ge(sOW[0].h, 16 * 32)
                gp.wait_ge(sOW[1].h, 16 * 32)
                gp.collective_compute(
                    "AllToAll", ALU.bypass, replica_groups=RG,
                    ins=[oconB.opt()], outs=[ogB.opt()])
                sAG.tag(gp)

            A2A_OB = sAG.n
            do_barrier()

        with contextlib.ExitStack() as S3:
            e3 = S3.enter_context
            oqfmB = e3(nc.sbuf_tensor("oqfmB", [P, 32 * 512], BF16))
            hidsl = [e3(nc.sbuf_tensor(f"hidsl{i}", [P, 512], F32))
                     for i in (0, 1)]
            x1st = [e3(nc.sbuf_tensor(f"x1st{i}", [P, 512], F32))
                    for i in (0, 1)]
            pso = [e3(nc.psum_tensor(f"pso{i}", [P, 512], F32)) for i in range(3)]

            sOFB = Sem(e3(nc.semaphore("sOFB")))
            sHS = [Sem(e3(nc.semaphore(f"sHS{i}"))) for i in (0, 1)]
            sPM = Sem(e3(nc.semaphore("sPM")))
            sXE = Sem(e3(nc.semaphore("sXE")))
            sXW = [Sem(e3(nc.semaphore(f"sXW{i}"))) for i in (0, 1)]

            @sec(block.sync, "sy")
            def _(sy):
                sy.wait_ge(sAG.h, A2A_OB)
                for fc in range(32):
                    i = fc // 4
                    f0 = (fc % 4) * P
                    sy.dma_start(out=oqfmB[:, fc * 512:(fc + 1) * 512],
                                 in_=ogB[i * 512:(i + 1) * 512, f0:f0 + P],
                                 transpose=True)
                    sOFB.tag(sy, 16)
                for n in range(8):
                    if n >= 2:
                        sy.wait_ge(sPM.h, (n - 1) * 8)
                        sy.dma_start(
                            out=woblk[n % 2][:],
                            in_=wg["o"][n * D:(n + 1) * D, :]
                            .rearrange("(fc p) n -> p fc n", p=P))
                        sWOB[n % 2].tag(sy, 16)
                    for t in range(8):
                        i = n * 8 + t
                        if i >= 2:
                            sy.wait_ge(sXE.h, i - 1)
                        sy.dma_start(out=hidsl[i % 2][:],
                                     in_=hid[t * P:(t + 1) * P,
                                             n * 512:(n + 1) * 512])
                        sHS[i % 2].tag(sy, 16)

            @sec(block.tensor, "te")
            def _(te):
                te.wait_ge(sOFA.h, 16 * 32)
                for n in range(8):
                    te.wait_ge(sWOB[n % 2].h, 16 * (n // 2 + 1))
                    for t in range(8):
                        i = n * 8 + t
                        if n == 0 and t == 4:
                            te.wait_ge(sOFB.h, 16 * 32)
                        if i >= 3:
                            te.wait_ge(sXE.h, i - 2)
                        for fc in range(32):
                            src_ap = (oqfmA[:, fc * 512 + t * P:
                                            fc * 512 + (t + 1) * P]
                                      if t < 4 else
                                      oqfmB[:, fc * 512 + (t - 4) * P:
                                            fc * 512 + (t - 3) * P])
                            te.matmul(pso[i % 3][:],
                                      src_ap,
                                      woblk[n % 2][:, fc, :],
                                      start=(fc == 0), stop=(fc == 31))
                        sPM.tag(te)

            @sec(block.vector, "ve")
            def _(ve):
                for i in range(64):
                    n, t = i // 8, i % 8
                    ve.wait_ge(sPM.h, i + 1)
                    ve.wait_ge(sHS[i % 2].h, 16 * (i // 2 + 1))
                    if i >= 2:
                        ve.wait_ge(sXW[i % 2].h, 16 * (i // 2))
                    ve.tensor_tensor(x1st[i % 2][:], pso[i % 3][:],
                                     hidsl[i % 2][:], ALU.add)
                    sXE.tag(ve)
                    ve.tensor_scalar(dmj[:], x1st[i % 2][:], 0.0, 0.0,
                                     ALU.add, ALU.add,
                                     accum_out=prtS[:, i % 2:i % 2 + 1])
                    ve.wait_ge(sSQ3.h, i + 1)
                    pS = prtS[:, i % 2:i % 2 + 1]
                    pQ = prtQ[:, i % 2:i % 2 + 1]
                    if n == 0:
                        ve.tensor_copy(stA[:, t:t + 1], pS)
                        ve.tensor_copy(stA[:, 8 + t:9 + t], pQ)
                    else:
                        ve.tensor_tensor(stA[:, t:t + 1], stA[:, t:t + 1],
                                         pS, ALU.add)
                        ve.tensor_tensor(stA[:, 8 + t:9 + t],
                                         stA[:, 8 + t:9 + t], pQ, ALU.add)
                    sST3.tag(ve)

            @sec(block.scalar, "sc")
            def _(sc):
                for n in range(8):
                    for t in range(8):
                        i = n * 8 + t
                        sc.wait_ge(sXE.h, i + 1)
                        if i >= 2:
                            sc.wait_ge(sST3.h, i - 1)
                        sc.activation(sqj[:], x1st[i % 2][:], AF.Square,
                                      accum_out=prtQ[:, i % 2:i % 2 + 1])
                        sSQ3.tag(sc)
                        sc.dma_start(out=x1raw_d[t * P:(t + 1) * P,
                                                 n * 512:(n + 1) * 512],
                                     in_=x1st[i % 2][:])
                        sXW[i % 2].tag(sc, 16)
                sc.wait_ge(sXW[0].h, 16 * 32)
                sc.wait_ge(sXW[1].h, 16 * 32)

            do_barrier()

        SP23.close()

        def ln_phase(src_d, dst_f32, dst_bf16, tag):
            with contextlib.ExitStack() as S4:
                e4 = S4.enter_context
                xr = [e4(nc.sbuf_tensor(f"xr{tag}{i}", [P, D], F32))
                      for i in (0, 1)]
                sqd = e4(nc.sbuf_tensor(f"sqd{tag}", [P, D], BF16))
                dmp = e4(nc.sbuf_tensor(f"dmp{tag}", [P, D], BF16))
                xo = [e4(nc.sbuf_tensor(f"xo{tag}{i}", [P, D], F32))
                      for i in (0, 1)]
                xb = ([e4(nc.sbuf_tensor(f"xb{tag}{i}", [P, D], BF16))
                       for i in (0, 1)] if dst_bf16 is not None else None)
                st_ = e4(nc.sbuf_tensor(f"st_{tag}", [P, 8], F32))
                sL4 = [Sem(e4(nc.semaphore(f"sL4{tag}{i}"))) for i in (0, 1)]
                sSQ = Sem(e4(nc.semaphore(f"sSQ{tag}")))
                sSS = Sem(e4(nc.semaphore(f"sSS{tag}")))
                sSD = Sem(e4(nc.semaphore(f"sSD{tag}")))
                sNM = Sem(e4(nc.semaphore(f"sNM{tag}")))
                sW4 = [Sem(e4(nc.semaphore(f"sW4{tag}{i}"))) for i in (0, 1)]
                nw = 2 if dst_bf16 is not None else 1

                @sec(block.sync, "sy")
                def _(sy):
                    for t in range(8):
                        if t >= 2:
                            sy.wait_ge(sNM.h, t - 1)
                        sy.dma_start(out=xr[t % 2][:],
                                     in_=src_d[t * P:(t + 1) * P, :])
                        sL4[t % 2].tag(sy, 16)

                @sec(block.scalar, "sc")
                def _(sc):
                    for t in range(8):
                        sc.wait_ge(sL4[t % 2].h, 16 * (t // 2 + 1))
                        sc.activation(sqd[:], xr[t % 2][:], AF.Square,
                                      accum_out=st_[:, 1:2])
                        sSQ.tag(sc)
                        sc.wait_ge(sSS.h, t + 1)
                        sc.sqrt(st_[:, 4:5], st_[:, 3:4])
                        sSD.tag(sc)
                        if t >= 1:
                            sc.wait_ge(sNM.h, t)
                            sc.dma_start(out=dst_f32[(t - 1) * P:t * P, :],
                                         in_=xo[(t - 1) % 2][:])
                            sW4[(t - 1) % 2].tag(sc, 16)
                            if dst_bf16 is not None:
                                sc.dma_start(
                                    out=dst_bf16[(t - 1) * P:t * P, :],
                                    in_=xb[(t - 1) % 2][:])
                                sW4[(t - 1) % 2].tag(sc, 16)
                    sc.wait_ge(sNM.h, 8)
                    sc.dma_start(out=dst_f32[7 * P:8 * P, :], in_=xo[1][:])
                    sW4[1].tag(sc, 16)
                    if dst_bf16 is not None:
                        sc.dma_start(out=dst_bf16[7 * P:8 * P, :],
                                     in_=xb[1][:])
                        sW4[1].tag(sc, 16)
                    sc.wait_ge(sW4[0].h, 16 * nw * 4)
                    sc.wait_ge(sW4[1].h, 16 * nw * 4)

                @sec(block.vector, "ve")
                def _(ve):
                    for t in range(8):
                        ve.wait_ge(sL4[t % 2].h, 16 * (t // 2 + 1))
                        ve.tensor_scalar(dmp[:], xr[t % 2][:], 0.0, 0.0,
                                         ALU.add, ALU.add,
                                         accum_out=st_[:, 0:1])
                        ve.wait_ge(sSQ.h, t + 1)
                        ve.tensor_scalar(st_[:, 2:3], st_[:, 0:1],
                                         1.0 / D, None, ALU.mult)
                        ve.tensor_scalar(st_[:, 6:7], st_[:, 1:2],
                                         1.0 / D, None, ALU.mult)
                        ve.tensor_tensor(st_[:, 7:8], st_[:, 2:3],
                                         st_[:, 2:3], ALU.mult)
                        ve.scalar_tensor_tensor(st_[:, 3:4], st_[:, 6:7],
                                                1e-5, st_[:, 7:8],
                                                ALU.add, ALU.subtract)
                        sSS.tag(ve)
                        ve.wait_ge(sSD.h, t + 1)
                        ve.reciprocal(st_[:, 5:6], st_[:, 4:5])
                        if t >= 2:
                            ve.wait_ge(sW4[t % 2].h, 16 * nw * ((t - 2) // 2 + 1))
                        ve.tensor_scalar(xo[t % 2][:], xr[t % 2][:],
                                         st_[:, 2:3], st_[:, 5:6],
                                         ALU.subtract, ALU.mult)
                        if dst_bf16 is not None:
                            ve.tensor_copy(xb[t % 2][:], xo[t % 2][:])
                        else:
                            ve.tensor_copy(xo[t % 2][:, 0:1],
                                           xo[t % 2][:, 0:1])
                        sNM.tag(ve)

                do_barrier()

        SP45 = ST.enter_context(contextlib.ExitStack())
        x1fm = SP45.enter_context(nc.sbuf_tensor("x1fm", [P, 32 * NT], BF16))
        sTE4 = Sem(SP45.enter_context(nc.semaphore("sTE4")))

        with contextlib.ExitStack() as S4a:
            e4 = S4a.enter_context
            xr4 = [e4(nc.sbuf_tensor(f"xr4{i}", [P, D], F32)) for i in (0, 1)]
            xo4 = [e4(nc.sbuf_tensor(f"xo4{i}", [P, D], F32)) for i in (0, 1)]
            xb4 = [e4(nc.sbuf_tensor(f"xb4{i}", [P, D], BF16)) for i in (0, 1)]
            tps4 = [e4(nc.psum_tensor(f"tps4{i}", [P, P], BF16))
                    for i in range(4)]
            sL4a = [Sem(e4(nc.semaphore(f"sL4a{i}"))) for i in (0, 1)]
            sNM4 = Sem(e4(nc.semaphore("sNM4")))
            sTP4 = Sem(e4(nc.semaphore("sTP4")))
            sW4a = [Sem(e4(nc.semaphore(f"sW4a{i}"))) for i in (0, 1)]
            sST4 = Sem(e4(nc.semaphore("sST4")))

            @sec(block.sync, "sy")
            def _(sy):
                for t in range(8):
                    if t >= 2:
                        sy.wait_ge(sNM4.h, t - 1)
                    sy.dma_start(out=xr4[t % 2][:],
                                 in_=x1raw_d[t * P:(t + 1) * P, :])
                    sL4a[t % 2].tag(sy, 16)

            @sec(block.vector, "ve")
            def _(ve):
                ve.tensor_scalar(stA[:, 16:24], stA[:, 0:8], 1.0 / D,
                                 None, ALU.mult)
                ve.tensor_scalar(stA[:, 8:16], stA[:, 8:16], 1.0 / D,
                                 None, ALU.mult)
                ve.tensor_tensor(stA[:, 24:32], stA[:, 16:24],
                                 stA[:, 16:24], ALU.mult)
                ve.scalar_tensor_tensor(stA[:, 8:16], stA[:, 8:16], 1e-5,
                                        stA[:, 24:32], ALU.add, ALU.subtract)
                sST4.tag(ve, to=1)
                ve.wait_ge(sST4.h, 2)
                ve.reciprocal(stA[:, 24:32], stA[:, 24:32])
                for t in range(8):
                    ve.wait_ge(sL4a[t % 2].h, 16 * (t // 2 + 1))
                    if t >= 2:
                        ve.wait_ge(sW4a[t % 2].h, 16 * ((t - 2) // 2 + 1))
                        ve.wait_ge(sTP4.h, 32 * (t - 1))
                    ve.tensor_scalar(xo4[t % 2][:], xr4[t % 2][:],
                                     stA[:, 16 + t:17 + t],
                                     stA[:, 24 + t:25 + t],
                                     ALU.subtract, ALU.mult)
                    ve.tensor_copy(xb4[t % 2][:], xo4[t % 2][:])
                    sNM4.tag(ve)

            @sec(block.scalar, "sc")
            def _(sc):
                sc.wait_ge(sST4.h, 1)
                sc.sqrt(stA[:, 24:32], stA[:, 8:16])
                sST4.tag(sc, to=2)
                for t in range(8):
                    sc.wait_ge(sNM4.h, t + 1)
                    sc.dma_start(out=x1_d[t * P:(t + 1) * P, :],
                                 in_=xo4[t % 2][:])
                    sW4a[t % 2].tag(sc, 16)
                    for fc in range(32):
                        j = t * 32 + fc
                        sc.wait_ge(sTP4.h, j + 1)
                        sc.copy(x1fm[:, fc * NT + t * P:
                                     fc * NT + (t + 1) * P],
                                tps4[j % 4][:])
                        sTE4.tag(sc)
                sc.wait_ge(sW4a[0].h, 16 * 4)
                sc.wait_ge(sW4a[1].h, 16 * 4)

            @sec(block.tensor, "te")
            def _(te):
                for t in range(8):
                    te.wait_ge(sNM4.h, t + 1)
                    for fc in range(32):
                        j = t * 32 + fc
                        if j >= 4:
                            te.wait_ge(sTE4.h, j - 3)
                        te.transpose(tps4[j % 4][:],
                                     xb4[t % 2][:, fc * P:(fc + 1) * P],
                                     ident[:])
                        sTP4.tag(te)

            do_barrier()

        with contextlib.ExitStack() as S5:
            e5 = S5.enter_context
            mcol = e5(nc.sbuf_tensor("mcol", [P, 32], F32))
            mcg = e5(nc.sbuf_tensor("mcg", [P, 32], F32))
            selE = e5(nc.sbuf_tensor("selE", [8, 512], F32))
            qcv = e5(nc.sbuf_tensor("qcv", [P, 1], F32))
            scv = e5(nc.sbuf_tensor("scv", [1, 16], F32))
            cT = e5(nc.sbuf_tensor("cT", [8, 1], F32))
            cvec = e5(nc.sbuf_tensor("cvec", [P, 4], F32))
            hsc = e5(nc.sbuf_tensor("hsc", [P, 4 * NT], BF16))

            sMR = Sem(e5(nc.semaphore("sMR")))
            sMW = Sem(e5(nc.semaphore("sMW")))
            sMWc = Sem(e5(nc.semaphore("sMWc")))
            sL5 = Sem(e5(nc.semaphore("sL5")))
            sL5g = Sem(e5(nc.semaphore("sL5g")))
            sHM = Sem(e5(nc.semaphore("sHM")))
            sHEv = Sem(e5(nc.semaphore("sHEv")))
            sCF = Sem(e5(nc.semaphore("sCF")))

            with contextlib.ExitStack() as S5a:
                e5a = S5a.enter_context
                ldsb = e5a(nc.sbuf_tensor("ldsb", [P, 32, 512], BF16))
                wqf = e5a(nc.sbuf_tensor("wqf", [P, 32, 128], F32))
                keyst = e5a(nc.sbuf_tensor("keyst", [P, 8], F32))
                hsb = e5a(nc.sbuf_tensor("hsb", [P, 4 * NT], F32))
                psq = e5a(nc.psum_tensor("psq", [P, 1], F32))
                pssc = e5a(nc.psum_tensor("pssc", [1, 8], F32))
                psct = e5a(nc.psum_tensor("psct", [8, 1], F32))
                pscv = e5a(nc.psum_tensor("pscv", [P, 1], F32))
                psh = [e5a(nc.psum_tensor(f"psh{i}", [P, 512], F32))
                       for i in (0, 1)]

                @sec(block.sync, "sy")
                def _(sy):
                    sy.dma_start(out=wqf[:].rearrange("p a b -> p (a b)"),
                                 in_=wqfT.rearrange("p a b -> p (a b)"))
                    sL5.tag(sy, 16)
                    sy.dma_start(out=keyst[:], in_=keysT)
                    sL5.tag(sy, 16)
                    sy.dma_start(out=selE[:], in_=selEc)
                    sL5.tag(sy, 16)

                @sec(block.gpsimd, "gp")
                def _(gp):
                    gp.dma_start(out=ldsb[:].rearrange("p a b -> p (a b)"),
                                 in_=ldh.rearrange("p a b -> p (a b)"))
                    sL5g.tag(gp, 16)
                    gp.wait_ge(sMR.h, 33)
                    gp.dma_start(out=mc_in, in_=mcol[:])
                    sMW.tag(gp, 16)
                    gp.wait_ge(sMW.h, 16)
                    gp.collective_compute(
                        "AllReduce", ALU.add, replica_groups=RG,
                        ins=[mc_in.opt()], outs=[mc_out.opt()])
                    sMWc.tag(gp, 1)
                    gp.wait_ge(sMWc.h, 1)
                    gp.dma_start(out=mcg[:], in_=mc_out)
                    sMW.tag(gp, 16)

                @sec(block.tensor, "te")
                def _(te):
                    te.wait_ge(sL5.h, 16 * 3)
                    te.wait_ge(sL5g.h, 16)
                    te.wait_ge(sTE4.h, 256)
                    for uc in range(4):
                        for tt in range(2):
                            i = uc * 2 + tt
                            if i >= 2:
                                te.wait_ge(sHEv.h, i - 1)
                            for fc in range(32):
                                te.matmul(
                                    psh[i % 2][:],
                                    ldsb[:, fc, uc * P:(uc + 1) * P],
                                    x1fm[:, fc * NT + tt * 512:
                                         fc * NT + (tt + 1) * 512],
                                    start=(fc == 0), stop=(fc == 31))
                            sHM.tag(te)
                    te.wait_ge(sMW.h, 32)
                    for fc in range(32):
                        te.matmul(psq[:], wqf[:, fc, :], mcg[:, fc:fc + 1],
                                  start=(fc == 0), stop=(fc == 31))
                    sCF.tag(te, to=1)
                    te.wait_ge(sCF.h, 2)
                    te.matmul(pssc[0:1, :], qcv[:], keyst[:],
                              start=True, stop=True)
                    sCF.tag(te, to=3)
                    te.wait_ge(sCF.h, 6)
                    te.matmul(psct[:], scv[0:1, 0:8], ccol[0:1, 4:5],
                              start=True, stop=True)
                    sCF.tag(te, to=7)
                    for uc in range(4):
                        te.wait_ge(sCF.h, 8 + 2 * uc)
                        te.matmul(pscv[:], selE[:, uc * P:(uc + 1) * P],
                                  cT[:], start=True, stop=True)
                        sCF.tag(te, to=9 + 2 * uc)

                @sec(block.vector, "ve")
                def _(ve):
                    ve.wait_ge(sTE4.h, 256)
                    for fc in range(32):
                        ve.reduce_sum(mcol[:, fc:fc + 1],
                                      x1fm[:, fc * NT:(fc + 1) * NT], AX.X)
                        sMR.tag(ve)
                    ve.tensor_scalar(mcol[:], mcol[:], 1.0 / S, None,
                                     ALU.mult)
                    sMR.tag(ve)
                    for i in range(8):
                        ve.wait_ge(sHM.h, i + 1)
                        uc, tt = i // 2, i % 2
                        ve.tensor_copy(hsb[:, uc * NT + tt * 512:
                                           uc * NT + (tt + 1) * 512],
                                       psh[i % 2][:])
                        sHEv.tag(ve)
                    ve.wait_ge(sCF.h, 1)
                    ve.tensor_copy(qcv[:], psq[:])
                    sCF.tag(ve, to=2)
                    ve.wait_ge(sCF.h, 3)
                    ve.tensor_copy(scv[0:1, 0:8], pssc[0:1, :])
                    ve.tensor_scalar(scv[0:1, 0:8], scv[0:1, 0:8], SSCL,
                                     None, ALU.mult)
                    ve.reduce_max(scv[0:1, 8:9], scv[0:1, 0:8], AX.X,
                                  negate=True)
                    sCF.tag(ve, to=4)
                    ve.wait_ge(sCF.h, 5)
                    ve.reduce_sum(scv[0:1, 9:10], scv[0:1, 0:8], AX.X)
                    ve.reciprocal(scv[0:1, 10:11], scv[0:1, 9:10])
                    ve.tensor_scalar(scv[0:1, 0:8], scv[0:1, 0:8],
                                     scv[0:1, 10:11], None, ALU.mult)
                    sCF.tag(ve, to=6)
                    ve.wait_ge(sCF.h, 7)
                    ve.tensor_copy(cT[:], psct[:])
                    sCF.tag(ve, to=8)
                    for uc in range(4):
                        ve.wait_ge(sCF.h, 9 + 2 * uc)
                        ve.tensor_copy(cvec[:, uc:uc + 1], pscv[:])
                        sCF.tag(ve, to=10 + 2 * uc)
                    ve.wait_ge(sHEv.h, 8)
                    for uc in range(4):
                        ve.tensor_scalar(hsc[:, uc * NT:(uc + 1) * NT],
                                         hsb[:, uc * NT:(uc + 1) * NT],
                                         cvec[:, uc:uc + 1], None, ALU.mult)
                    sCF.tag(ve, to=17)

                @sec(block.scalar, "sc")
                def _(sc):
                    sc.wait_ge(sCF.h, 4)
                    sc.activation(scv[0:1, 0:8], scv[0:1, 0:8], AF.Exp,
                                  bias=scv[0:1, 8:9])
                    sCF.tag(sc, to=5)

            CF_DONE = 17

            with contextlib.ExitStack() as S5b:
                e5b = S5b.enter_context
                wob = [e5b(nc.sbuf_tensor(f"wob{i}", [P, 32, 512], BF16))
                       for i in (0, 1)]
                usb = [e5b(nc.sbuf_tensor(f"usb{i}", [P, 4, 512], BF16))
                       for i in (0, 1)]
                x1re = [e5b(nc.sbuf_tensor(f"x1re{i}", [P, 512], F32))
                        for i in (0, 1)]
                x2st = [e5b(nc.sbuf_tensor(f"x2st{i}", [P, 512], F32))
                        for i in (0, 1)]
                psf = [e5b(nc.psum_tensor(f"psf{i}", [P, 512], F32))
                       for i in range(3)]

                sWB5 = [Sem(e5b(nc.semaphore(f"sWB5{i}"))) for i in (0, 1)]
                sXR5 = [Sem(e5b(nc.semaphore(f"sXR5{i}"))) for i in (0, 1)]
                sPM5 = Sem(e5b(nc.semaphore("sPM5")))
                sXE5 = Sem(e5b(nc.semaphore("sXE5")))
                sXW5 = [Sem(e5b(nc.semaphore(f"sXW5{i}"))) for i in (0, 1)]

                @sec(block.gpsimd, "gp")
                def _(gp):
                    gp.wait_ge(sCF.h, CF_DONE)
                    for n in range(8):
                        if n >= 2:
                            gp.wait_ge(sPM5.h, (n - 1) * 8)
                        gp.dma_start(
                            out=wob[n % 2][:],
                            in_=worgT.rearrange("(fc p) o -> p fc o", p=P)
                            [:, :, n * 512:(n + 1) * 512])
                        sWB5[n % 2].tag(gp, 16)
                        gp.dma_start(out=usb[n % 2][:],
                                     in_=uh[:, :, n * 512:(n + 1) * 512])
                        sWB5[n % 2].tag(gp, 16)

                @sec(block.sync, "sy")
                def _(sy):
                    sy.wait_ge(sCF.h, CF_DONE)
                    for n in range(8):
                        for t in range(8):
                            i = n * 8 + t
                            if i >= 2:
                                sy.wait_ge(sXE5.h, i - 1)
                            sy.dma_start(
                                out=x1re[i % 2][:],
                                in_=x1_d[t * P:(t + 1) * P,
                                         n * 512:(n + 1) * 512])
                            sXR5[i % 2].tag(sy, 16)

                @sec(block.tensor, "te")
                def _(te):
                    te.wait_ge(sCF.h, CF_DONE)
                    for n in range(8):
                        te.wait_ge(sWB5[n % 2].h, 32 * (n // 2 + 1))
                        for t in range(8):
                            i = n * 8 + t
                            if i >= 3:
                                te.wait_ge(sXE5.h, i - 2)
                            for fc in range(32):
                                te.matmul(psf[i % 3][:],
                                          x1fm[:, fc * NT + t * P:
                                               fc * NT + (t + 1) * P],
                                          wob[n % 2][:, fc, :],
                                          start=(fc == 0), stop=False)
                            for uc in range(4):
                                te.matmul(psf[i % 3][:],
                                          hsc[:, uc * NT + t * P:
                                              uc * NT + (t + 1) * P],
                                          usb[n % 2][:, uc, :],
                                          start=False, stop=(uc == 3))
                            sPM5.tag(te)

                @sec(block.vector, "ve")
                def _(ve):
                    for i in range(64):
                        ve.wait_ge(sPM5.h, i + 1)
                        ve.wait_ge(sXR5[i % 2].h, 16 * (i // 2 + 1))
                        if i >= 2:
                            ve.wait_ge(sXW5[i % 2].h, 16 * (i // 2))
                        ve.tensor_tensor(x2st[i % 2][:], psf[i % 3][:],
                                         x1re[i % 2][:], ALU.add)
                        sXE5.tag(ve)

                @sec(block.scalar, "sc")
                def _(sc):
                    for n in range(8):
                        for t in range(8):
                            i = n * 8 + t
                            sc.wait_ge(sXE5.h, i + 1)
                            sc.dma_start(
                                out=x2raw_d[t * P:(t + 1) * P,
                                            n * 512:(n + 1) * 512],
                                in_=x2st[i % 2][:])
                            sXW5[i % 2].tag(sc, 16)
                    sc.wait_ge(sXW5[0].h, 16 * 32)
                    sc.wait_ge(sXW5[1].h, 16 * 32)

                do_barrier()

        SP45.close()

        ln_phase(x2raw_d, out, None, "b")

    return nc


def _host_inputs(inputs):
    X = np.ascontiguousarray(np.asarray(inputs["L0_attn_hidden_states"],
                                        np.float32).reshape(S, D))
    Wq = np.asarray(inputs["L0_attn_Wq"], np.float32)
    Wk = np.asarray(inputs["L0_attn_Wk"], np.float32)
    Wv = np.asarray(inputs["L0_attn_Wv"], np.float32)
    Wo = np.asarray(inputs["L0_attn_Wo"], np.float32)
    Worg = np.asarray(inputs["L0_ffn_W_org"], np.float32)
    ld = np.asarray(inputs["L0_ffn_lora_down"], np.float32)
    lu = np.asarray(inputs["L0_ffn_lora_up"], np.float32)
    lk = np.asarray(inputs["L0_ffn_lora_keys"], np.float32)
    Wqf = np.asarray(inputs["L0_ffn_Wq"], np.float32)

    worgT = np.ascontiguousarray(Worg.T)
    ldh = np.ascontiguousarray(
        ld.transpose(1, 0, 2).reshape(32, 128, 512).transpose(1, 0, 2))
    uh = np.ascontiguousarray(
        lu.reshape(512, D).reshape(4, 128, D).transpose(1, 0, 2))
    wqfT = np.ascontiguousarray(
        Wqf.T.reshape(32, 128, 128).transpose(1, 0, 2))
    keysT = np.ascontiguousarray(lk.T)

    tric = np.where(np.arange(P)[None, :] >= np.arange(P)[:, None],
                    0.0, MASKV).astype(np.float32)
    gidx = np.arange(128)
    mq_g = np.where(gidx < 64, 7.0, np.where(gidx < 96, 31.0, 127.0))
    mqvc = np.broadcast_to(mq_g.astype(np.float32), (P, P)).copy()
    imqvc = np.broadcast_to((1.0 / mq_g).astype(np.float32), (P, P)).copy()
    identc = np.eye(P, dtype=np.float32).astype(BF)
    selEc = np.zeros((8, 512), np.float32)
    for uc in range(4):
        for p in range(128):
            selEc[2 * uc + p // 64, uc * 128 + p] = 1.0

    inv = (1.0 / (10000.0 ** (np.arange(0, HD, 2, dtype=np.float64) / HD))
           ).astype(np.float32)

    maps = []
    for c in range(W):
        pos = np.arange(c * NT, (c + 1) * NT, dtype=np.float32)
        ang = (pos[None, :] * inv[:, None]).astype(np.float32)
        cc_half = np.cos(ang.astype(np.float64)).astype(np.float32)
        ss_half = np.sin(ang.astype(np.float64)).astype(np.float32)
        cct = np.concatenate([cc_half, cc_half], axis=0)
        sst = np.concatenate([ss_half, -ss_half], axis=0)
        mq_o = 7.0 if c < 4 else (31.0 if c < 6 else 127.0)
        ccolc = np.zeros((P, 8), np.float32)
        ccolc[:, 0] = SSCL
        ccolc[:, 1] = EXPC
        ccolc[:, 2] = mq_o
        ccolc[:, 3] = 1.0 / mq_o
        ccolc[:, 4] = 1.0
        ccolc[:, 5] = 0.0
        maps.append({
            "hid": np.ascontiguousarray(X[c * NT:(c + 1) * NT]),
            "wkS": np.ascontiguousarray(Wk[c * 512:(c + 1) * 512]),
            "wqS": np.ascontiguousarray(Wq[c * 512:(c + 1) * 512]),
            "wvS": np.ascontiguousarray(Wv[c * 512:(c + 1) * 512]),
            "woS": np.ascontiguousarray(Wo[c * 512:(c + 1) * 512]),
            "worgT": worgT,
            "ldh": ldh,
            "uh": uh,
            "wqfT": wqfT,
            "keysT": keysT,
            "cctab": np.ascontiguousarray(cct),
            "sstab": np.ascontiguousarray(sst),
            "tric": tric,
            "mqvc": mqvc,
            "imqvc": imqvc,
            "identc": identc,
            "selEc": selEc,
            "ccolc": ccolc,
        })
    return maps


def kernel(**inputs):
    maps = _host_inputs(inputs)
    with allow_ici_all_to_all():
        nc = build_nc()
    res = run_bass_kernel_spmd(nc, maps, core_ids=list(range(W)))
    outp = np.empty((S, D), np.float32)
    for c in range(W):
        outp[c * NT:(c + 1) * NT] = np.asarray(res.results[c]["out"],
                                               np.float32)
    return outp.reshape(1, S, D)


# version 14 (final)
# speedup vs baseline: 1.0316x; 1.0316x over previous
import contextlib
import math
import sys

import numpy as np

try:
    import concourse.bass as bass
except ImportError:
    sys.path.insert(0, "/opt/trn_rl_repo")
    import concourse.bass as bass

import ml_dtypes
import concourse.mybir as mybir
from concourse.bass_utils import run_bass_kernel_spmd
from concourse.aot_env import allow_ici_all_to_all

F32 = mybir.dt.float32
BF16 = mybir.dt.bfloat16
AF = mybir.ActivationFunctionType
ALU = mybir.AluOpType
AX = mybir.AxisListType
BF = ml_dtypes.bfloat16

W = 8
S = 8192
D = 4096
HD = 128
NT = 1024
HL = 4
P = 128
MAGIC = 12582912.0
EXPC = -26.0
SSCL = 1.0 / math.sqrt(128.0)
MASKV = -1.0e9
VW = 132
GEPS = 1e-30

GORDER = [0, 2, 4, 6, 8, 10, 12, 14, 1, 3, 5, 7, 9, 11, 13, 15]

_CHAIN_OPS = frozenset((
    "tensor_tensor", "tensor_add", "tensor_sub", "tensor_mul", "tensor_max",
    "tensor_scalar", "tensor_single_scalar", "tensor_scalar_add",
    "tensor_scalar_sub", "tensor_scalar_mul", "tensor_scalar_max",
    "tensor_scalar_min", "tensor_copy", "tensor_reduce", "reduce_sum",
    "reduce_max", "scalar_tensor_tensor", "memset", "memzero", "reciprocal",
    "activation", "copy", "mul", "add", "sqrt", "square", "select",
))
_RECORD_OPS = frozenset(("matmul", "transpose", "dma_start",
                         "collective_compute"))


class Eng:

    def __init__(self, eng, st):
        object.__setattr__(self, "_eng", eng)
        object.__setattr__(self, "_st", st)

    def __getattr__(self, name):
        target = getattr(self._eng, name)
        if name in _CHAIN_OPS:
            def wrap(*a, **k):
                st = self._st
                if st["last"] is not None:
                    if st["sync"] is None:
                        st["n"] += 1
                        st["last"].then_inc(st["h"], 1)
                        st["sync"] = (st["h"], st["n"])
                    self._eng.wait_ge(*st["sync"])
                ins = target(*a, **k)
                st["last"] = ins
                st["sync"] = None
                return ins
            return wrap
        if name in _RECORD_OPS:
            def wrap2(*a, **k):
                ins = target(*a, **k)
                self._st["last"] = ins
                self._st["sync"] = None
                return ins
            return wrap2
        return target


class Sem:
    __slots__ = ("h", "n")

    def __init__(self, h):
        self.h = h
        self.n = 0

    def tag(self, eng, k=1, to=None):
        st = eng._st
        st["last"].then_inc(self.h, k)
        self.n += k
        st["sync"] = (self.h, to if to is not None else self.n)
        return self.n


def _attn_sched():
    blocks = []
    loads = []
    for gi, g in enumerate(GORDER):
        for s in range(g + 1):
            loads.append(dict(gi=gi, g=g, s=s, li=len(loads),
                              lastblk=len(blocks) + 3))
            for h in range(HL):
                diag = (s == g)
                avs = [list(range(4)) if not diag else list(range(qt + 1))
                       for qt in range(4)]
                blocks.append(dict(b=len(blocks), gi=gi, g=g, s=s, h=h,
                                   diag=diag, avs=avs, li=len(loads) - 1))
    cummask = []
    cnt = 0
    for blk in blocks:
        for u in (0, 1):
            if blk["diag"]:
                cnt += 2
            cummask.append(cnt)
    ginfo = []
    for gi, g in enumerate(GORDER):
        blks = [b for b in blocks if b["gi"] == gi]
        ginfo.append(dict(gi=gi, g=g, first=blks[0]["b"], last=blks[-1]["b"]))
    return blocks, loads, cummask, ginfo


def build_nc():
    nc = bass.Bass("TRN2", target_bir_lowering=False, debug=False)

    hid = nc.dram_tensor("hid", [NT, D], F32, kind="ExternalInput").ap()
    wkS = nc.dram_tensor("wkS", [512, D], F32, kind="ExternalInput").ap()
    wqS = nc.dram_tensor("wqS", [512, D], F32, kind="ExternalInput").ap()
    wvS = nc.dram_tensor("wvS", [512, D], F32, kind="ExternalInput").ap()
    woS = nc.dram_tensor("woS", [512, D], F32, kind="ExternalInput").ap()
    worgT = nc.dram_tensor("worgT", [D, D], F32, kind="ExternalInput").ap()
    ldh = nc.dram_tensor("ldh", [P, 32, 512], F32, kind="ExternalInput").ap()
    uh = nc.dram_tensor("uh", [P, 4, D], F32, kind="ExternalInput").ap()
    wqfT = nc.dram_tensor("wqfT", [P, 32, 128], F32, kind="ExternalInput").ap()
    keysT = nc.dram_tensor("keysT", [P, 8], F32, kind="ExternalInput").ap()
    cctab = nc.dram_tensor("cctab", [P, NT], F32, kind="ExternalInput").ap()
    sstab = nc.dram_tensor("sstab", [P, NT], F32, kind="ExternalInput").ap()
    tric = nc.dram_tensor("tric", [P, P], F32, kind="ExternalInput").ap()
    mqvc = nc.dram_tensor("mqvc", [P, P], F32, kind="ExternalInput").ap()
    imqvc = nc.dram_tensor("imqvc", [P, P], F32, kind="ExternalInput").ap()
    identc = nc.dram_tensor("identc", [P, P], BF16, kind="ExternalInput").ap()
    selEc = nc.dram_tensor("selEc", [8, 512], F32, kind="ExternalInput").ap()
    ccolc = nc.dram_tensor("ccolc", [P, 8], F32, kind="ExternalInput").ap()
    out = nc.dram_tensor("out", [NT, D], F32, kind="ExternalOutput").ap()

    wc = {n: nc.dram_tensor(f"wc_{n}", [D, 512], BF16).ap() for n in "kqvo"}
    wg = {n: nc.dram_tensor(f"wg_{n}", [W * D, 512], BF16).ap() for n in "kqvo"}
    kcon = nc.dram_tensor("kcon", [D, NT], BF16).ap()
    qcon = nc.dram_tensor("qcon", [D, NT], BF16).ap()
    vcon = nc.dram_tensor("vcon", [S, HL * VW], BF16).ap()
    kgd = nc.dram_tensor("kgd", [D, NT], BF16).ap()
    qgd = nc.dram_tensor("qgd", [D, NT], BF16).ap()
    vgd = nc.dram_tensor("vgd", [S, HL * VW], BF16).ap()
    oconA = nc.dram_tensor("oconA", [W * 512, 512], BF16).ap()
    oconB = nc.dram_tensor("oconB", [W * 512, 512], BF16).ap()
    ogA = nc.dram_tensor("ogA", [W * 512, 512], BF16).ap()
    ogB = nc.dram_tensor("ogB", [W * 512, 512], BF16).ap()
    x1raw_d = nc.dram_tensor("x1raw_d", [NT, D], F32).ap()
    x1_d = nc.dram_tensor("x1_d", [NT, D], F32).ap()
    x2raw_d = nc.dram_tensor("x2raw_d", [NT, D], F32).ap()
    mc_in = nc.dram_tensor("mc_in", [P, 32], F32).ap()
    mc_out = nc.dram_tensor("mc_out", [P, 32], F32).ap()

    RG = [list(range(W))]

    with contextlib.ExitStack() as ST:
        e = ST.enter_context
        ccol = e(nc.sbuf_tensor("ccol", [P, 8], F32))
        tri = e(nc.sbuf_tensor("tri", [P, P], F32))
        mqv = e(nc.sbuf_tensor("mqv", [P, P], F32))
        imqv = e(nc.sbuf_tensor("imqv", [P, P], F32))
        ident = e(nc.sbuf_tensor("ident", [P, P], BF16))
        stA = e(nc.sbuf_tensor("stA", [P, 32], F32))
        prtS = e(nc.sbuf_tensor("prtS", [P, 2], F32))
        prtQ = e(nc.sbuf_tensor("prtQ", [P, 2], F32))
        dmj = e(nc.sbuf_tensor("dmj", [P, 512], BF16))
        sqj = e(nc.sbuf_tensor("sqj", [P, 512], BF16))
        stB = e(nc.sbuf_tensor("stB", [P, 32], F32))
        sSQ3 = Sem(e(nc.semaphore("sSQ3")))
        sST3 = Sem(e(nc.semaphore("sST3")))

        sB = Sem(e(nc.semaphore("sB")))
        sAG = Sem(e(nc.semaphore("sAG")))
        sLC = Sem(e(nc.semaphore("sLC")))
        sWS = Sem(e(nc.semaphore("sWS")))
        chain = {}
        for en in ("sy", "te", "sc", "ve", "gp"):
            chain[en] = {"h": e(nc.semaphore(f"chain_{en}")), "n": 0,
                         "last": None, "sync": None}

        block = e(nc.Block())
        phase_no = [0]

        def sec(emitter, en):
            def reg(f):
                emitter(lambda eng: f(Eng(eng, chain[en])))
            return reg

        def do_barrier(skip_gp=False):
            phase_no[0] += 1
            tgt = 5 * phase_no[0]
            emits = [block.sync, block.tensor, block.scalar, block.vector]
            if not skip_gp:
                emits.append(block.gpsimd)
            for emit in emits:
                def _f(eng, _t=tgt):
                    eng.nop().then_inc(sB.h, 1)
                    eng.wait_ge(sB.h, _t)
                emit(_f)
            sB.n = tgt

        def quant_emit(ve, x, qout, scr, gmax, ginv, gs, FD,
                       mq=None, imq=None, mqc=None, imqc=None):
            G = FD // 32
            xg = x.rearrange("p (g k) -> p g k", k=32)
            ve.tensor_reduce(gmax, xg, AX.X, ALU.max, apply_absolute_value=True)
            ve.tensor_scalar(ginv, gmax, GEPS, None, ALU.max)
            ve.reciprocal(ginv, ginv)
            if mq is not None:
                ve.tensor_tensor(ginv, ginv, mq, ALU.mult)
                ve.tensor_tensor(gs, gmax, imq, ALU.mult)
            else:
                ve.tensor_scalar(ginv, ginv, mqc, None, ALU.mult)
                ve.tensor_scalar(gs, gmax, imqc, None, ALU.mult)
            gb = ginv.rearrange("p (g o) -> p g o", o=1).broadcast_to([P, G, 32])
            sb_ = gs.rearrange("p (g o) -> p g o", o=1).broadcast_to([P, G, 32])
            scrg = scr.rearrange("p (g k) -> p g k", k=32)
            ve.tensor_tensor(scrg, xg, gb, ALU.mult)
            ve.tensor_scalar(scr, scr, MAGIC, -MAGIC, ALU.add, ALU.add)
            ve.tensor_tensor(qout.rearrange("p (g k) -> p g k", k=32),
                             scrg, sb_, ALU.mult)

        SP01 = ST.enter_context(contextlib.ExitStack())
        qhid_fm = SP01.enter_context(
            nc.sbuf_tensor("qhid_fm", [P, 32 * NT], BF16))
        sTE = Sem(SP01.enter_context(nc.semaphore("sTE")))

        with contextlib.ExitStack() as S0:
            e0 = S0.enter_context
            wtile = [e0(nc.sbuf_tensor(f"wtile{i}", [P, D], F32)) for i in (0, 1)]
            wq16 = [e0(nc.sbuf_tensor(f"wq16{i}", [P, D], BF16))
                    for i in (0, 1)]
            scr0 = e0(nc.sbuf_tensor("scr0", [P, D], F32))
            gmax0 = e0(nc.sbuf_tensor("gmax0", [P, P], F32))
            ginv0 = e0(nc.sbuf_tensor("ginv0", [P, P], F32))
            gs0 = e0(nc.sbuf_tensor("gs0", [P, P], F32))
            wstg = e0(nc.sbuf_tensor("wstg", [P, 32, 512], BF16))
            hq16 = [e0(nc.sbuf_tensor(f"hq16{i}", [P, D], BF16)) for i in (0, 1)]
            tps = [e0(nc.psum_tensor(f"tps{i}", [P, P], BF16)) for i in range(4)]

            sWL = [Sem(e0(nc.semaphore(f"sWL{i}"))) for i in (0, 1)]
            sQZ = Sem(e0(nc.semaphore("sQZ")))
            sTP = Sem(e0(nc.semaphore("sTP")))
            sHQ = Sem(e0(nc.semaphore("sHQ")))

            worder = "kqvo"
            wsrc = {"k": wkS, "q": wqS, "v": wvS, "o": woS}
            AG_W = {wn: wi + 1 for wi, wn in enumerate(worder)}

            @sec(block.sync, "sy")
            def _(sy):
                for t, src in ((ccol, ccolc), (tri, tric), (mqv, mqvc),
                               (imqv, imqvc), (ident, identc)):
                    sy.dma_start(out=t[:], in_=src)
                    sLC.tag(sy, 16)
                for wi, wn in enumerate(worder):
                    sy.wait_ge(sTE.h, (wi + 1) * 128)
                    sy.dma_start(
                        out=wc[wn].rearrange("(fc p) r -> p fc r", p=P),
                        in_=wstg[:])
                    sWS.tag(sy, 16)
                sy.wait_ge(sWS.h, 16 * 4)

            @sec(block.vector, "ve")
            def _(ve):
                ve.wait_ge(sLC.h, 16 * 5)
                for idx in range(24):
                    ve.wait_ge(sWL[idx % 2].h, 16 * (idx // 2 + 1))
                    if idx < 16:
                        if idx > 1:
                            ve.wait_ge(sTP.h, (idx - 1) * 32)
                        quant_emit(ve, wtile[idx % 2][:], wq16[idx % 2][:],
                                   scr0[:], gmax0[:], ginv0[:], gs0[:], D,
                                   mq=mqv[:], imq=imqv[:])
                        sQZ.tag(ve)
                    else:
                        t = idx - 16
                        if t >= 2:
                            ve.wait_ge(sTP.h, 512 + (t - 1) * 32)
                        quant_emit(ve, wtile[idx % 2][:], hq16[t % 2][:],
                                   scr0[:], gmax0[:], ginv0[:], gs0[:], D,
                                   mq=mqv[:], imq=imqv[:])
                        sHQ.tag(ve)

            @sec(block.scalar, "sc")
            def _(sc):
                srcs = ([wsrc[wn][rt * P:(rt + 1) * P, :]
                         for wn in worder for rt in range(4)] +
                        [hid[t * P:(t + 1) * P, :] for t in range(8)])
                for k in (0, 1):
                    sc.dma_start(out=wtile[k][:], in_=srcs[k])
                    sWL[k].tag(sc, 16)
                for wi, wn in enumerate(worder):
                    for rt in range(4):
                        i = wi * 4 + rt
                        if rt == 0 and wi > 0:
                            sc.wait_ge(sWS.h, 16 * wi)
                        for fc in range(32):
                            j = i * 32 + fc
                            sc.wait_ge(sTP.h, j + 1)
                            sc.copy(wstg[:, fc, rt * P:(rt + 1) * P],
                                    tps[j % 4][:])
                            sTE.tag(sc)
                            if fc == 0 and i + 2 < 24:
                                sc.dma_start(out=wtile[i % 2][:],
                                             in_=srcs[i + 2])
                                sWL[i % 2].tag(sc, 16)
                for t in range(8):
                    for fc in range(32):
                        j = 512 + t * 32 + fc
                        sc.wait_ge(sTP.h, j + 1)
                        sc.copy(qhid_fm[:, fc * NT + t * P:
                                        fc * NT + (t + 1) * P],
                                tps[j % 4][:])
                        sTE.tag(sc)
                        if fc == 0 and 18 + t < 24:
                            sc.dma_start(out=wtile[t % 2][:],
                                         in_=srcs[18 + t])
                            sWL[t % 2].tag(sc, 16)

            @sec(block.tensor, "te")
            def _(te):
                te.wait_ge(sLC.h, 16 * 5)
                for wi, wn in enumerate(worder):
                    for rt in range(4):
                        i = wi * 4 + rt
                        te.wait_ge(sQZ.h, i + 1)
                        for fc in range(32):
                            j = i * 32 + fc
                            if j >= 4:
                                te.wait_ge(sTE.h, j - 3)
                            te.transpose(tps[j % 4][:],
                                         wq16[i % 2][:, fc * P:(fc + 1) * P],
                                         ident[:])
                            sTP.tag(te)
                for t in range(8):
                    te.wait_ge(sHQ.h, t + 1)
                    for fc in range(32):
                        j = 512 + t * 32 + fc
                        te.wait_ge(sTE.h, j - 3)
                        te.transpose(tps[j % 4][:],
                                     hq16[t % 2][:, fc * P:(fc + 1) * P],
                                     ident[:])
                        sTP.tag(te)

            @sec(block.gpsimd, "gp")
            def _(gp):
                gp.nop().then_inc(sB.h, 1)
                for wi, wn in enumerate("kqv"):
                    gp.wait_ge(sWS.h, 16 * (wi + 1))
                    gp.collective_compute(
                        "AllGather", ALU.bypass, replica_groups=RG,
                        ins=[wc[wn].opt()], outs=[wg[wn].opt()])
                    sAG.tag(gp)

            do_barrier(skip_gp=True)

        with contextlib.ExitStack() as S1:
            e1 = S1.enter_context
            cct = e1(nc.sbuf_tensor("cct", [P, NT], F32))
            sst = e1(nc.sbuf_tensor("sst", [P, NT], F32))
            wblk = [e1(nc.sbuf_tensor(f"wblk{i}", [P, 32, 512], BF16))
                    for i in range(3)]
            ropeA = e1(nc.sbuf_tensor("ropeA", [P, 512], F32))
            ropeB = e1(nc.sbuf_tensor("ropeB", [P, 512], F32))
            ropeS = e1(nc.sbuf_tensor("ropeS", [P, 512], F32))
            kstg = [e1(nc.sbuf_tensor(f"kstg{i}", [P, 512], BF16)) for i in (0, 1)]
            qbst = [e1(nc.sbuf_tensor(f"qbst{i}", [P, 512], BF16)) for i in (0, 1)]
            vstg = [e1(nc.sbuf_tensor(f"vstg{i}", [P, HL * VW], BF16))
                    for i in (0, 1)]
            pps = [e1(nc.psum_tensor(f"pps{i}", [P, 512], F32)) for i in range(3)]

            sCT = Sem(e1(nc.semaphore("sCT")))
            sKWB = [Sem(e1(nc.semaphore(f"sKWB{i}"))) for i in (0, 1, 2)]
            sVWB = [Sem(e1(nc.semaphore(f"sVWB{i}"))) for i in (0, 1, 2)]
            sPK = Sem(e1(nc.semaphore("sPK")))
            sRKv = Sem(e1(nc.semaphore("sRKv")))
            sVE = Sem(e1(nc.semaphore("sVE")))
            sVI = Sem(e1(nc.semaphore("sVI")))
            sCNK = [Sem(e1(nc.semaphore(f"sCNK{i}"))) for i in (0, 1)]
            sCNQ = [Sem(e1(nc.semaphore(f"sCNQ{i}"))) for i in (0, 1)]
            sCNV = [Sem(e1(nc.semaphore(f"sCNV{i}"))) for i in (0, 1)]

            @sec(block.sync, "sy")
            def _(sy):
                sy.dma_start(out=cct[:], in_=cctab)
                sCT.tag(sy, 16)
                sy.dma_start(out=sst[:], in_=sstab)
                sCT.tag(sy, 16)
                for m in range(32):
                    if m % 4 == 0:
                        rb = m // 4
                        if rb >= 3:
                            sy.wait_ge(sPK.h, 8 * (rb - 2))
                        sy.wait_ge(sAG.h, AG_W["k"])
                        sy.dma_start(
                            out=wblk[rb % 3][:],
                            in_=wg["k"][rb * D:(rb + 1) * D, :]
                            .rearrange("(fc p) n -> p fc n", p=P))
                        sKWB[rb % 3].tag(sy, 16)
                    if m >= 1:
                        for tt in range(2):
                            i = (m - 1) * 2 + tt
                            sy.wait_ge(sRKv.h, i + 1)
                            sy.dma_start(
                                out=kcon[(m - 1) * P:m * P,
                                         tt * 512:(tt + 1) * 512],
                                in_=kstg[i % 2][:])
                            sCNK[i % 2].tag(sy, 16)
                for tt in range(2):
                    i = 31 * 2 + tt
                    sy.wait_ge(sRKv.h, i + 1)
                    sy.dma_start(out=kcon[31 * P:32 * P,
                                          tt * 512:(tt + 1) * 512],
                                 in_=kstg[i % 2][:])
                    sCNK[i % 2].tag(sy, 16)
                for m in range(32):
                    if m % 4 == 0:
                        rb = m // 4
                        j = 8 + rb
                        if rb >= 3:
                            sy.wait_ge(sPK.h, 64 + 8 * (rb - 2))
                        else:
                            sy.wait_ge(sPK.h, 64 - 16 + 8 * rb)
                        sy.wait_ge(sAG.h, AG_W["q"])
                        sy.dma_start(
                            out=wblk[j % 3][:],
                            in_=wg["q"][rb * D:(rb + 1) * D, :]
                            .rearrange("(fc p) n -> p fc n", p=P))
                        sKWB[j % 3].tag(sy, 16)
                    if m >= 1:
                        for tt in range(2):
                            qi = (m - 1) * 2 + tt
                            sy.wait_ge(sRKv.h, 64 + qi + 1)
                            sy.dma_start(
                                out=qcon[(m - 1) * P:m * P,
                                         tt * 512:(tt + 1) * 512],
                                in_=qbst[qi % 2][:])
                            sCNQ[qi % 2].tag(sy, 16)
                for tt in range(2):
                    qi = 31 * 2 + tt
                    sy.wait_ge(sRKv.h, 64 + qi + 1)
                    sy.dma_start(out=qcon[31 * P:32 * P,
                                          tt * 512:(tt + 1) * 512],
                                 in_=qbst[qi % 2][:])
                    sCNQ[qi % 2].tag(sy, 16)
                for n in range(8):
                    j = 16 + n
                    if n >= 3:
                        sy.wait_ge(sPK.h, 128 + (n - 2) * 8)
                    else:
                        sy.wait_ge(sPK.h, 128 - 16 + 8 * n)
                    sy.wait_ge(sAG.h, AG_W["v"])
                    sy.dma_start(
                        out=wblk[j % 3][:],
                        in_=wg["v"][n * D:(n + 1) * D, :]
                        .rearrange("(fc p) n -> p fc n", p=P))
                    sVWB[n % 3].tag(sy, 16)
                    if n >= 1:
                        for tt in range(8):
                            vi = (n - 1) * 8 + tt
                            sy.wait_ge(sVI.h, 2)
                            sy.wait_ge(sVE.h, vi + 1)
                            sy.dma_start(
                                out=vcon[(n - 1) * NT + tt * P:
                                         (n - 1) * NT + (tt + 1) * P, :],
                                in_=vstg[vi % 2][:])
                            sCNV[vi % 2].tag(sy, 16)
                for tt in range(8):
                    vi = 7 * 8 + tt
                    sy.wait_ge(sVE.h, vi + 1)
                    sy.dma_start(
                        out=vcon[7 * NT + tt * P: 7 * NT + (tt + 1) * P, :],
                        in_=vstg[vi % 2][:])
                    sCNV[vi % 2].tag(sy, 16)
                sy.wait_ge(sCNK[0].h, 16 * 32)
                sy.wait_ge(sCNK[1].h, 16 * 32)
                sy.wait_ge(sCNQ[0].h, 16 * 32)
                sy.wait_ge(sCNQ[1].h, 16 * 32)
                sy.wait_ge(sCNV[0].h, 16 * 32)
                sy.wait_ge(sCNV[1].h, 16 * 32)

            @sec(block.tensor, "te")
            def _(te):
                te.wait_ge(sTE.h, 768)
                for base, kind in ((0, "k"), (64, "q")):
                    for m in range(32):
                        for tt in range(2):
                            i = base + m * 2 + tt
                            li = m * 2 + tt
                            j = (0 if kind == "k" else 8) + m // 4
                            if tt == 0 and m % 4 == 0:
                                te.wait_ge(sKWB[j % 3].h,
                                           16 * (j // 3 + 1))
                            if kind == "k":
                                if i >= 3:
                                    te.wait_ge(sRKv.h, i - 2)
                            else:
                                te.wait_ge(sRKv.h, 62 + li)
                            for fc in range(32):
                                te.matmul(
                                    pps[i % 3][:],
                                    wblk[j % 3][:, fc,
                                               (m % 4) * P:(m % 4 + 1) * P],
                                    qhid_fm[:, fc * NT + tt * 512:
                                            fc * NT + (tt + 1) * 512],
                                    start=(fc == 0), stop=(fc == 31))
                            sPK.tag(te)
                for n in range(8):
                    j = 16 + n
                    for tt in range(8):
                        vi = n * 8 + tt
                        i = 128 + vi
                        if tt == 0:
                            te.wait_ge(sVWB[n % 3].h, 16 * (n // 3 + 1))
                        if vi < 3:
                            te.wait_ge(sRKv.h, 126 + vi)
                        else:
                            te.wait_ge(sVE.h, vi - 2)
                        for fc in range(32):
                            te.matmul(
                                pps[i % 3][:],
                                qhid_fm[:, fc * NT + tt * P:
                                        fc * NT + (tt + 1) * P],
                                wblk[j % 3][:, fc, :],
                                start=(fc == 0), stop=(fc == 31))
                        sPK.tag(te)

            @sec(block.vector, "ve")
            def _(ve):
                ve.wait_ge(sCT.h, 32)
                for i in range(64):
                    tt = i % 2
                    ve.wait_ge(sPK.h, i + 1)
                    if i >= 2:
                        ve.wait_ge(sCNK[i % 2].h, 16 * (i // 2))
                    c = cct[:, tt * 512:(tt + 1) * 512]
                    s2 = sst[:, tt * 512:(tt + 1) * 512]
                    ps = pps[i % 3][:]
                    ve.tensor_tensor(ropeA[:], ps, c, ALU.mult)
                    ve.tensor_tensor(ropeB[:], ps, s2, ALU.mult)
                    ve.tensor_copy(ropeS[0:64, :], ropeB[64:128, :])
                    ve.tensor_copy(ropeS[64:128, :], ropeB[0:64, :])
                    ve.tensor_tensor(kstg[i % 2][:], ropeA[:], ropeS[:],
                                     ALU.add)
                    sRKv.tag(ve)
                for qi in range(64):
                    tt = qi % 2
                    i = 64 + qi
                    ve.wait_ge(sPK.h, i + 1)
                    if qi >= 2:
                        ve.wait_ge(sCNQ[qi % 2].h, 16 * (qi // 2))
                    c = cct[:, tt * 512:(tt + 1) * 512]
                    s2 = sst[:, tt * 512:(tt + 1) * 512]
                    ps = pps[i % 3][:]
                    ve.tensor_tensor(ropeA[:], ps, c, ALU.mult)
                    ve.tensor_tensor(ropeB[:], ps, s2, ALU.mult)
                    ve.tensor_copy(ropeS[0:64, :], ropeB[64:128, :])
                    ve.tensor_copy(ropeS[64:128, :], ropeB[0:64, :])
                    ve.tensor_tensor(qbst[qi % 2][:], ropeA[:], ropeS[:],
                                     ALU.add)
                    sRKv.tag(ve)
                for bi in (0, 1):
                    ve.memset(vstg[bi][:], 0.0)
                    ve.memset(vstg[bi][:].rearrange("p (h c) -> p h c", c=VW)
                              [:, :, 128:129], 1.0)
                    sVI.tag(ve)
                for n in range(8):
                    for tt in range(8):
                        vi = n * 8 + tt
                        i = 128 + vi
                        ve.wait_ge(sPK.h, i + 1)
                        if vi >= 2:
                            ve.wait_ge(sCNV[vi % 2].h, 16 * (vi // 2))
                        for hl in range(HL):
                            ve.tensor_copy(
                                vstg[vi % 2][:, hl * VW:hl * VW + P],
                                pps[i % 3][:, hl * P:(hl + 1) * P])
                        sVE.tag(ve)

            @sec(block.gpsimd, "gp")
            def _(gp):
                gp.wait_ge(sCNK[0].h, 16 * 32)
                gp.wait_ge(sCNK[1].h, 16 * 32)
                gp.collective_compute(
                    "AllToAll", ALU.bypass, replica_groups=RG,
                    ins=[kcon.opt()], outs=[kgd.opt()])
                sAG.tag(gp)
                gp.wait_ge(sCNQ[0].h, 16 * 32)
                gp.wait_ge(sCNQ[1].h, 16 * 32)
                gp.collective_compute(
                    "AllToAll", ALU.bypass, replica_groups=RG,
                    ins=[qcon.opt()], outs=[qgd.opt()])
                sAG.tag(gp)
                gp.wait_ge(sCNV[0].h, 16 * 32)
                gp.wait_ge(sCNV[1].h, 16 * 32)
                gp.collective_compute(
                    "AllToAll", ALU.bypass, replica_groups=RG,
                    ins=[vcon.opt()], outs=[vgd.opt()])
                sAG.tag(gp)
                gp.wait_ge(sWS.h, 16 * 4)
                gp.collective_compute(
                    "AllGather", ALU.bypass, replica_groups=RG,
                    ins=[wc["o"].opt()], outs=[wg["o"].opt()])
                sAG.tag(gp)

            A2A_V = sAG.n - 1
            AG_W["o"] = sAG.n
            A2A_OA = sAG.n + 1
            do_barrier()

        SP01.close()

        blocks, loads, cummask, ginfo = _attn_sched()

        SP23 = ST.enter_context(contextlib.ExitStack())
        oqfmA = SP23.enter_context(
            nc.sbuf_tensor("oqfmA", [P, 32 * 512], BF16))
        woblk = [SP23.enter_context(
            nc.sbuf_tensor(f"woblk{i}", [P, 32, 512], BF16)) for i in (0, 1)]
        sOFA = Sem(SP23.enter_context(nc.semaphore("sOFA")))
        sWOB = [Sem(SP23.enter_context(nc.semaphore(f"sWOB{i}")))
                for i in (0, 1)]

        with contextlib.ExitStack() as S2:
            e2 = S2.enter_context
            ksg = [e2(nc.sbuf_tensor(f"ksg{i}", [P, HL * 512], BF16))
                   for i in (0, 1)]
            vsg = [e2(nc.sbuf_tensor(f"vsg{i}", [P, 4 * HL * VW], BF16))
                   for i in (0, 1)]
            qsb = [e2(nc.sbuf_tensor(f"qsb{i}", [P, HL * 512], BF16))
                   for i in (0, 1)]
            AC = [e2(nc.sbuf_tensor(f"AC{i}", [P, 2 * HL * 264], F32))
                  for i in (0, 1)]
            expS = [e2(nc.sbuf_tensor(f"expS{i}", [P, 2048], BF16))
                    for i in range(4)]
            rinv = [e2(nc.sbuf_tensor(f"rinv{i}", [P, 16], F32)) for i in (0, 1)]
            onrm = [e2(nc.sbuf_tensor(f"onrm{i}", [P, 512], F32)) for i in (0, 1)]
            oqs = [e2(nc.sbuf_tensor(f"oqs{i}", [P, 512], BF16)) for i in (0, 1)]
            og4 = e2(nc.sbuf_tensor("og4", [P, 16], F32))
            og4i = e2(nc.sbuf_tensor("og4i", [P, 16], F32))
            og4s = e2(nc.sbuf_tensor("og4s", [P, 16], F32))
            oscr = e2(nc.sbuf_tensor("oscr", [P, 512], F32))
            psS = e2(nc.psum_tensor("psS", [P, 2048], F32))
            acc = [[e2(nc.psum_tensor(f"acc{pr}_{pa}", [P, 264], F32))
                    for pa in (0, 1)] for pr in (0, 1)]

            sKS = [Sem(e2(nc.semaphore(f"sKS{i}"))) for i in (0, 1)]
            sQL = [Sem(e2(nc.semaphore(f"sQL{i}"))) for i in (0, 1)]
            sSM = Sem(e2(nc.semaphore("sSM")))
            sMK = Sem(e2(nc.semaphore("sMK")))
            sEX = Sem(e2(nc.semaphore("sEX")))
            sAV = Sem(e2(nc.semaphore("sAV")))
            sACA = Sem(e2(nc.semaphore("sACA")))
            sRV = Sem(e2(nc.semaphore("sRV")))
            sOE = Sem(e2(nc.semaphore("sOE")))
            sQZ2 = Sem(e2(nc.semaphore("sQZ2")))
            sOW = [Sem(e2(nc.semaphore(f"sOW{i}"))) for i in (0, 1)]

            def tok_addr(g512):
                return g512 // 2, (g512 % 2) * 512

            @sec(block.sync, "sy")
            def _(sy):
                ofa_q = [("w", 0), ("w", 1)] + [("a", fc) for fc in range(32)]

                def emit_ofa():
                    kind_, a_ = ofa_q.pop(0)
                    if kind_ == "w":
                        sy.wait_ge(sAG.h, AG_W["o"])
                        sy.dma_start(
                            out=woblk[a_][:],
                            in_=wg["o"][a_ * D:(a_ + 1) * D, :]
                            .rearrange("(fc p) n -> p fc n", p=P))
                        sWOB[a_].tag(sy, 16)
                    else:
                        if a_ == 0:
                            sy.wait_ge(sAG.h, A2A_OA)
                        i2 = a_ // 4
                        f0 = (a_ % 4) * P
                        sy.dma_start(
                            out=oqfmA[:, a_ * 512:(a_ + 1) * 512],
                            in_=ogA[i2 * 512:(i2 + 1) * 512, f0:f0 + P],
                            transpose=True)
                        sOFA.tag(sy, 16)

                sy.wait_ge(sAG.h, A2A_V)
                for gi, g in enumerate(GORDER):
                    for s in range(g + 1):
                        ld = [l for l in loads
                              if l["gi"] == gi and l["s"] == s][0]
                        li = ld["li"]
                        if s == 0:
                            if gi >= 2:
                                sy.wait_ge(sSM.h,
                                           2 * (ginfo[gi - 2]["last"] + 1))
                            i, c0 = tok_addr(g)
                            sy.dma_start(
                                out=qsb[gi % 2][:].rearrange(
                                    "p (h t) -> p h t", t=512),
                                in_=qgd[i * 512:(i + 1) * 512, c0:c0 + 512]
                                .rearrange("(h p) t -> p h t", p=P))
                            sQL[gi % 2].tag(sy, 16)
                        if li >= 2:
                            sy.wait_ge(sAV.h, loads[li - 2]["lastblk"] + 1)
                        i, c0 = tok_addr(s)
                        sy.dma_start(
                            out=ksg[li % 2][:].rearrange(
                                "p (h t) -> p h t", t=512),
                            in_=kgd[i * 512:(i + 1) * 512, c0:c0 + 512]
                            .rearrange("(h p) t -> p h t", p=P))
                        sKS[li % 2].tag(sy, 16)
                        sy.dma_start(
                            out=vsg[li % 2][:].rearrange(
                                "p (a c) -> p a c", c=HL * VW),
                            in_=vgd[s * 512:(s + 1) * 512, :]
                            .rearrange("(a p) c -> p a c", p=P))
                        sKS[li % 2].tag(sy, 16)
                        if s == 1 and gi >= 1:
                            pg = gi - 1
                            gq = GORDER[pg]
                            dst = oconA if (gq % 2 == 0) else oconB
                            j = gq // 2
                            for qt in range(4):
                                oi = pg * 4 + qt
                                sy.wait_ge(sQZ2.h, oi + 1)
                                sy.dma_start(
                                    out=dst[j * 512 + qt * P:
                                            j * 512 + (qt + 1) * P, :],
                                    in_=oqs[oi % 2][:])
                                sOW[oi % 2].tag(sy, 16)
                        if gi >= 11 and ofa_q:
                            emit_ofa()
                gq = GORDER[15]
                dst = oconA if (gq % 2 == 0) else oconB
                j = gq // 2
                for qt in range(4):
                    oi = 15 * 4 + qt
                    sy.wait_ge(sQZ2.h, oi + 1)
                    sy.dma_start(
                        out=dst[j * 512 + qt * P: j * 512 + (qt + 1) * P, :],
                        in_=oqs[oi % 2][:])
                    sOW[oi % 2].tag(sy, 16)
                while ofa_q:
                    emit_ofa()
                sy.wait_ge(sOW[0].h, 16 * 32)
                sy.wait_ge(sOW[1].h, 16 * 32)

            @sec(block.tensor, "te")
            def _(te):
                def emit_avs(blk):
                    b = blk["b"]
                    buf = blk["li"] % 2
                    h = blk["h"]
                    te.wait_ge(sEX.h, 2 * b + 2)
                    if b >= 2:
                        te.wait_ge(sACA.h, b - 1)
                    par = b % 2
                    for qt in range(4):
                        kts = blk["avs"][qt]
                        for ki, kt in enumerate(kts):
                            u, ksl = kt // 2, kt % 2
                            te.matmul(
                                acc[par][qt // 2][:, (qt % 2) * 132:
                                                  (qt % 2) * 132 + 132],
                                expS[b % 4][:, u * 1024 + ksl * 512 + qt * P:
                                            u * 1024 + ksl * 512 +
                                            (qt + 1) * P],
                                vsg[buf][:, kt * (HL * VW) + h * VW:
                                         kt * (HL * VW) + h * VW + 132],
                                start=(ki == 0), stop=(ki == len(kts) - 1))
                    sAV.tag(te)

                pend = None
                for blk in blocks:
                    b, gi, s, h, li = (blk["b"], blk["gi"], blk["s"],
                                       blk["h"], blk["li"])
                    buf = li % 2
                    if h == 0:
                        te.wait_ge(sKS[li % 2].h, 32 * (li // 2 + 1))
                        if s == 0:
                            te.wait_ge(sQL[gi % 2].h, 16 * (gi // 2 + 1))
                    for u in (0, 1):
                        U = 2 * b + u
                        if U >= 2:
                            te.wait_ge(sEX.h, U - 1)
                        for ksl in (0, 1):
                            kt = u * 2 + ksl
                            te.matmul(
                                psS[:, u * 1024 + ksl * 512:
                                    u * 1024 + (ksl + 1) * 512],
                                ksg[buf][:, h * 512 + kt * P:
                                         h * 512 + (kt + 1) * P],
                                qsb[gi % 2][:, h * 512:(h + 1) * 512],
                                start=True, stop=True)
                        sSM.tag(te)
                    if pend is not None:
                        emit_avs(pend)
                    pend = blk
                emit_avs(pend)

            @sec(block.scalar, "sc")
            def _(sc):
                sc.wait_ge(sLC.h, 16 * 5)

                def emit_evacs(pg, qt):
                    if qt == 0:
                        sc.wait_ge(sRV.h, pg + 1)
                    oi = pg * 4 + qt
                    if oi >= 2:
                        sc.wait_ge(sQZ2.h, oi - 1)
                    for h in range(HL):
                        c0 = (h * 2 + qt // 2) * 264 + (qt % 2) * 132
                        a = (h * 2 + qt // 2) * 2 + (qt % 2)
                        sc.mul(onrm[oi % 2][:, h * P:(h + 1) * P],
                               AC[pg % 2][:, c0:c0 + 128],
                               rinv[pg % 2][:, a:a + 1])
                        sOE.tag(sc)

                for gi, g in enumerate(GORDER):
                    gblocks = [x for x in blocks if x["gi"] == gi]
                    for bi_, blk in enumerate(gblocks):
                        b = blk["b"]
                        for u in (0, 1):
                            U = 2 * b + u
                            sc.wait_ge(sSM.h, U + 1)
                            if cummask[U] > 0:
                                sc.wait_ge(sMK.h, cummask[U])
                            if b >= 4:
                                sc.wait_ge(sAV.h, b - 3)
                            sc.activation(
                                expS[b % 4][:, u * 1024:(u + 1) * 1024],
                                psS[:, u * 1024:(u + 1) * 1024], AF.Exp,
                                bias=ccol[:, 1:2], scale=ccol[:, 0:1])
                            sEX.tag(sc)
                        if bi_ <= 3 and gi >= 1:
                            emit_evacs(gi - 1, bi_)
                for qt in range(4):
                    emit_evacs(15, qt)

            @sec(block.vector, "ve")
            def _(ve):
                for gi, g in enumerate(GORDER):
                    if gi >= 2:
                        ve.wait_ge(sOE.h, 16 * (gi - 1))
                    gblocks = [x for x in blocks if x["gi"] == gi]
                    for bi_, blk in enumerate(gblocks):
                        b, s, h = blk["b"], blk["s"], blk["h"]
                        if blk["diag"]:
                            for kt in range(4):
                                u, ksl = kt // 2, kt % 2
                                ve.wait_ge(sSM.h, 2 * b + u + 1)
                                ve.tensor_tensor(
                                    psS[:, u * 1024 + ksl * 512 + kt * P:
                                        u * 1024 + ksl * 512 + (kt + 1) * P],
                                    psS[:, u * 1024 + ksl * 512 + kt * P:
                                        u * 1024 + ksl * 512 + (kt + 1) * P],
                                    tri[:], ALU.add)
                                sMK.tag(ve)
                        ve.wait_ge(sAV.h, b + 1)
                        par = b % 2
                        for pa in (0, 1):
                            dst = AC[gi % 2][:, (h * 2 + pa) * 264:
                                             (h * 2 + pa + 1) * 264]
                            if s == 0:
                                ve.tensor_copy(dst, acc[par][pa][:])
                            else:
                                ve.tensor_tensor(dst, dst, acc[par][pa][:],
                                                 ALU.add)
                        sACA.tag(ve)
                        if bi_ <= 3 and gi >= 1:
                            pg, qt = gi - 1, bi_
                            oi = pg * 4 + qt
                            ve.wait_ge(sOE.h, pg * 16 + (qt + 1) * 4)
                            if oi >= 2:
                                ve.wait_ge(sOW[oi % 2].h, 16 * (oi // 2))
                            quant_emit(ve, onrm[oi % 2][:],
                                       oqs[oi % 2][:], oscr[:], og4[:],
                                       og4i[:], og4s[:], 512,
                                       mqc=ccol[:, 2:3],
                                       imqc=ccol[:, 3:4])
                            sQZ2.tag(ve)
                    if gi >= 2:
                        ve.wait_ge(sOE.h, 16 * (gi - 1))
                    rsv = AC[gi % 2][:].rearrange("p (a c) -> p a c", c=132)
                    ve.reciprocal(rinv[gi % 2][:], rsv[:, :, 128:129])
                    sRV.tag(ve)
                for qt in range(4):
                    oi = 15 * 4 + qt
                    ve.wait_ge(sOE.h, 15 * 16 + (qt + 1) * 4)
                    if oi >= 2:
                        ve.wait_ge(sOW[oi % 2].h, 16 * (oi // 2))
                    quant_emit(ve, onrm[oi % 2][:], oqs[oi % 2][:],
                               oscr[:], og4[:], og4i[:], og4s[:], 512,
                               mqc=ccol[:, 2:3], imqc=ccol[:, 3:4])
                    sQZ2.tag(ve)

            @sec(block.gpsimd, "gp")
            def _(gp):
                gp.wait_ge(sOW[0].h, 16 * 16)
                gp.wait_ge(sOW[1].h, 16 * 16)
                gp.collective_compute(
                    "AllToAll", ALU.bypass, replica_groups=RG,
                    ins=[oconA.opt()], outs=[ogA.opt()])
                sAG.tag(gp)
                gp.wait_ge(sOW[0].h, 16 * 32)
                gp.wait_ge(sOW[1].h, 16 * 32)
                gp.collective_compute(
                    "AllToAll", ALU.bypass, replica_groups=RG,
                    ins=[oconB.opt()], outs=[ogB.opt()])
                sAG.tag(gp)

            A2A_OB = sAG.n
            do_barrier()

        with contextlib.ExitStack() as S3:
            e3 = S3.enter_context
            oqfmB = e3(nc.sbuf_tensor("oqfmB", [P, 32 * 512], BF16))
            hidsl = [e3(nc.sbuf_tensor(f"hidsl{i}", [P, 512], F32))
                     for i in (0, 1)]
            x1st = [e3(nc.sbuf_tensor(f"x1st{i}", [P, 512], F32))
                    for i in (0, 1)]
            pso = [e3(nc.psum_tensor(f"pso{i}", [P, 512], F32)) for i in range(3)]

            sOFB = Sem(e3(nc.semaphore("sOFB")))
            sHS = [Sem(e3(nc.semaphore(f"sHS{i}"))) for i in (0, 1)]
            sPM = Sem(e3(nc.semaphore("sPM")))
            sXE = Sem(e3(nc.semaphore("sXE")))
            sXW = [Sem(e3(nc.semaphore(f"sXW{i}"))) for i in (0, 1)]

            @sec(block.sync, "sy")
            def _(sy):
                sy.wait_ge(sAG.h, A2A_OB)
                for fc in range(32):
                    i = fc // 4
                    f0 = (fc % 4) * P
                    sy.dma_start(out=oqfmB[:, fc * 512:(fc + 1) * 512],
                                 in_=ogB[i * 512:(i + 1) * 512, f0:f0 + P],
                                 transpose=True)
                    sOFB.tag(sy, 16)
                for n in range(8):
                    if n >= 2:
                        sy.wait_ge(sPM.h, (n - 1) * 8)
                        sy.dma_start(
                            out=woblk[n % 2][:],
                            in_=wg["o"][n * D:(n + 1) * D, :]
                            .rearrange("(fc p) n -> p fc n", p=P))
                        sWOB[n % 2].tag(sy, 16)
                    for t in range(8):
                        i = n * 8 + t
                        if i >= 2:
                            sy.wait_ge(sXE.h, i - 1)
                        sy.dma_start(out=hidsl[i % 2][:],
                                     in_=hid[t * P:(t + 1) * P,
                                             n * 512:(n + 1) * 512])
                        sHS[i % 2].tag(sy, 16)

            @sec(block.tensor, "te")
            def _(te):
                te.wait_ge(sOFA.h, 16 * 32)
                for n in range(8):
                    te.wait_ge(sWOB[n % 2].h, 16 * (n // 2 + 1))
                    for t in range(8):
                        i = n * 8 + t
                        if n == 0 and t == 4:
                            te.wait_ge(sOFB.h, 16 * 32)
                        if i >= 3:
                            te.wait_ge(sXE.h, i - 2)
                        for fc in range(32):
                            src_ap = (oqfmA[:, fc * 512 + t * P:
                                            fc * 512 + (t + 1) * P]
                                      if t < 4 else
                                      oqfmB[:, fc * 512 + (t - 4) * P:
                                            fc * 512 + (t - 3) * P])
                            te.matmul(pso[i % 3][:],
                                      src_ap,
                                      woblk[n % 2][:, fc, :],
                                      start=(fc == 0), stop=(fc == 31))
                        sPM.tag(te)

            @sec(block.vector, "ve")
            def _(ve):
                for i in range(64):
                    n, t = i // 8, i % 8
                    ve.wait_ge(sPM.h, i + 1)
                    ve.wait_ge(sHS[i % 2].h, 16 * (i // 2 + 1))
                    if i >= 2:
                        ve.wait_ge(sXW[i % 2].h, 16 * (i // 2))
                    ve.tensor_tensor(x1st[i % 2][:], pso[i % 3][:],
                                     hidsl[i % 2][:], ALU.add)
                    sXE.tag(ve)
                    ve.tensor_scalar(dmj[:], x1st[i % 2][:], 0.0, 0.0,
                                     ALU.add, ALU.add,
                                     accum_out=prtS[:, i % 2:i % 2 + 1])
                    ve.wait_ge(sSQ3.h, i + 1)
                    pS = prtS[:, i % 2:i % 2 + 1]
                    pQ = prtQ[:, i % 2:i % 2 + 1]
                    if n == 0:
                        ve.tensor_copy(stA[:, t:t + 1], pS)
                        ve.tensor_copy(stA[:, 8 + t:9 + t], pQ)
                    else:
                        ve.tensor_tensor(stA[:, t:t + 1], stA[:, t:t + 1],
                                         pS, ALU.add)
                        ve.tensor_tensor(stA[:, 8 + t:9 + t],
                                         stA[:, 8 + t:9 + t], pQ, ALU.add)
                    sST3.tag(ve)

            @sec(block.scalar, "sc")
            def _(sc):
                for n in range(8):
                    for t in range(8):
                        i = n * 8 + t
                        sc.wait_ge(sXE.h, i + 1)
                        if i >= 2:
                            sc.wait_ge(sST3.h, i - 1)
                        sc.activation(sqj[:], x1st[i % 2][:], AF.Square,
                                      accum_out=prtQ[:, i % 2:i % 2 + 1])
                        sSQ3.tag(sc)
                        sc.dma_start(out=x1raw_d[t * P:(t + 1) * P,
                                                 n * 512:(n + 1) * 512],
                                     in_=x1st[i % 2][:])
                        sXW[i % 2].tag(sc, 16)
                sc.wait_ge(sXW[0].h, 16 * 32)
                sc.wait_ge(sXW[1].h, 16 * 32)

            do_barrier()

        SP23.close()

        def ln_phase(src_d, dst_f32, dst_bf16, tag):
            with contextlib.ExitStack() as S4:
                e4 = S4.enter_context
                xr = [e4(nc.sbuf_tensor(f"xr{tag}{i}", [P, D], F32))
                      for i in (0, 1)]
                sqd = e4(nc.sbuf_tensor(f"sqd{tag}", [P, D], BF16))
                dmp = e4(nc.sbuf_tensor(f"dmp{tag}", [P, D], BF16))
                xo = [e4(nc.sbuf_tensor(f"xo{tag}{i}", [P, D], F32))
                      for i in (0, 1)]
                xb = ([e4(nc.sbuf_tensor(f"xb{tag}{i}", [P, D], BF16))
                       for i in (0, 1)] if dst_bf16 is not None else None)
                st_ = e4(nc.sbuf_tensor(f"st_{tag}", [P, 8], F32))
                sL4 = [Sem(e4(nc.semaphore(f"sL4{tag}{i}"))) for i in (0, 1)]
                sSQ = Sem(e4(nc.semaphore(f"sSQ{tag}")))
                sSS = Sem(e4(nc.semaphore(f"sSS{tag}")))
                sSD = Sem(e4(nc.semaphore(f"sSD{tag}")))
                sNM = Sem(e4(nc.semaphore(f"sNM{tag}")))
                sW4 = [Sem(e4(nc.semaphore(f"sW4{tag}{i}"))) for i in (0, 1)]
                nw = 2 if dst_bf16 is not None else 1

                @sec(block.sync, "sy")
                def _(sy):
                    for t in range(8):
                        if t >= 2:
                            sy.wait_ge(sNM.h, t - 1)
                        sy.dma_start(out=xr[t % 2][:],
                                     in_=src_d[t * P:(t + 1) * P, :])
                        sL4[t % 2].tag(sy, 16)

                @sec(block.scalar, "sc")
                def _(sc):
                    for t in range(8):
                        sc.wait_ge(sL4[t % 2].h, 16 * (t // 2 + 1))
                        sc.activation(sqd[:], xr[t % 2][:], AF.Square,
                                      accum_out=st_[:, 1:2])
                        sSQ.tag(sc)
                        sc.wait_ge(sSS.h, t + 1)
                        sc.sqrt(st_[:, 4:5], st_[:, 3:4])
                        sSD.tag(sc)
                        if t >= 1:
                            sc.wait_ge(sNM.h, t)
                            sc.dma_start(out=dst_f32[(t - 1) * P:t * P, :],
                                         in_=xo[(t - 1) % 2][:])
                            sW4[(t - 1) % 2].tag(sc, 16)
                            if dst_bf16 is not None:
                                sc.dma_start(
                                    out=dst_bf16[(t - 1) * P:t * P, :],
                                    in_=xb[(t - 1) % 2][:])
                                sW4[(t - 1) % 2].tag(sc, 16)
                    sc.wait_ge(sNM.h, 8)
                    sc.dma_start(out=dst_f32[7 * P:8 * P, :], in_=xo[1][:])
                    sW4[1].tag(sc, 16)
                    if dst_bf16 is not None:
                        sc.dma_start(out=dst_bf16[7 * P:8 * P, :],
                                     in_=xb[1][:])
                        sW4[1].tag(sc, 16)
                    sc.wait_ge(sW4[0].h, 16 * nw * 4)
                    sc.wait_ge(sW4[1].h, 16 * nw * 4)

                @sec(block.vector, "ve")
                def _(ve):
                    for t in range(8):
                        ve.wait_ge(sL4[t % 2].h, 16 * (t // 2 + 1))
                        ve.tensor_scalar(dmp[:], xr[t % 2][:], 0.0, 0.0,
                                         ALU.add, ALU.add,
                                         accum_out=st_[:, 0:1])
                        ve.wait_ge(sSQ.h, t + 1)
                        ve.tensor_scalar(st_[:, 2:3], st_[:, 0:1],
                                         1.0 / D, None, ALU.mult)
                        ve.tensor_scalar(st_[:, 6:7], st_[:, 1:2],
                                         1.0 / D, None, ALU.mult)
                        ve.tensor_tensor(st_[:, 7:8], st_[:, 2:3],
                                         st_[:, 2:3], ALU.mult)
                        ve.scalar_tensor_tensor(st_[:, 3:4], st_[:, 6:7],
                                                1e-5, st_[:, 7:8],
                                                ALU.add, ALU.subtract)
                        sSS.tag(ve)
                        ve.wait_ge(sSD.h, t + 1)
                        ve.reciprocal(st_[:, 5:6], st_[:, 4:5])
                        if t >= 2:
                            ve.wait_ge(sW4[t % 2].h, 16 * nw * ((t - 2) // 2 + 1))
                        ve.tensor_scalar(xo[t % 2][:], xr[t % 2][:],
                                         st_[:, 2:3], st_[:, 5:6],
                                         ALU.subtract, ALU.mult)
                        if dst_bf16 is not None:
                            ve.tensor_copy(xb[t % 2][:], xo[t % 2][:])
                        else:
                            ve.tensor_copy(xo[t % 2][:, 0:1],
                                           xo[t % 2][:, 0:1])
                        sNM.tag(ve)

                do_barrier()

        SP45 = ST.enter_context(contextlib.ExitStack())
        x1fm = SP45.enter_context(nc.sbuf_tensor("x1fm", [P, 32 * NT], BF16))
        sTE4 = Sem(SP45.enter_context(nc.semaphore("sTE4")))

        with contextlib.ExitStack() as S4a:
            e4 = S4a.enter_context
            xr4 = [e4(nc.sbuf_tensor(f"xr4{i}", [P, D], F32)) for i in (0, 1)]
            xo4 = [e4(nc.sbuf_tensor(f"xo4{i}", [P, D], F32)) for i in (0, 1)]
            xb4 = [e4(nc.sbuf_tensor(f"xb4{i}", [P, D], BF16)) for i in (0, 1)]
            tps4 = [e4(nc.psum_tensor(f"tps4{i}", [P, P], BF16))
                    for i in range(4)]
            sL4a = [Sem(e4(nc.semaphore(f"sL4a{i}"))) for i in (0, 1)]
            sNM4 = Sem(e4(nc.semaphore("sNM4")))
            sTP4 = Sem(e4(nc.semaphore("sTP4")))
            sW4a = [Sem(e4(nc.semaphore(f"sW4a{i}"))) for i in (0, 1)]
            sST4 = Sem(e4(nc.semaphore("sST4")))

            @sec(block.sync, "sy")
            def _(sy):
                for t in range(8):
                    if t >= 2:
                        sy.wait_ge(sNM4.h, t - 1)
                    sy.dma_start(out=xr4[t % 2][:],
                                 in_=x1raw_d[t * P:(t + 1) * P, :])
                    sL4a[t % 2].tag(sy, 16)

            @sec(block.vector, "ve")
            def _(ve):
                ve.tensor_scalar(stA[:, 16:24], stA[:, 0:8], 1.0 / D,
                                 None, ALU.mult)
                ve.tensor_scalar(stA[:, 8:16], stA[:, 8:16], 1.0 / D,
                                 None, ALU.mult)
                ve.tensor_tensor(stA[:, 24:32], stA[:, 16:24],
                                 stA[:, 16:24], ALU.mult)
                ve.scalar_tensor_tensor(stA[:, 8:16], stA[:, 8:16], 1e-5,
                                        stA[:, 24:32], ALU.add, ALU.subtract)
                sST4.tag(ve, to=1)
                ve.wait_ge(sST4.h, 2)
                ve.reciprocal(stA[:, 24:32], stA[:, 24:32])
                for t in range(8):
                    ve.wait_ge(sL4a[t % 2].h, 16 * (t // 2 + 1))
                    if t >= 2:
                        ve.wait_ge(sW4a[t % 2].h, 16 * ((t - 2) // 2 + 1))
                        ve.wait_ge(sTP4.h, 32 * (t - 1))
                    ve.tensor_scalar(xo4[t % 2][:], xr4[t % 2][:],
                                     stA[:, 16 + t:17 + t],
                                     stA[:, 24 + t:25 + t],
                                     ALU.subtract, ALU.mult)
                    ve.tensor_copy(xb4[t % 2][:], xo4[t % 2][:])
                    sNM4.tag(ve)

            @sec(block.scalar, "sc")
            def _(sc):
                sc.wait_ge(sST4.h, 1)
                sc.sqrt(stA[:, 24:32], stA[:, 8:16])
                sST4.tag(sc, to=2)
                for t in range(8):
                    sc.wait_ge(sNM4.h, t + 1)
                    sc.dma_start(out=x1_d[t * P:(t + 1) * P, :],
                                 in_=xo4[t % 2][:])
                    sW4a[t % 2].tag(sc, 16)
                    for fc in range(32):
                        j = t * 32 + fc
                        sc.wait_ge(sTP4.h, j + 1)
                        sc.copy(x1fm[:, fc * NT + t * P:
                                     fc * NT + (t + 1) * P],
                                tps4[j % 4][:])
                        sTE4.tag(sc)
                sc.wait_ge(sW4a[0].h, 16 * 4)
                sc.wait_ge(sW4a[1].h, 16 * 4)

            @sec(block.tensor, "te")
            def _(te):
                for t in range(8):
                    te.wait_ge(sNM4.h, t + 1)
                    for fc in range(32):
                        j = t * 32 + fc
                        if j >= 4:
                            te.wait_ge(sTE4.h, j - 3)
                        te.transpose(tps4[j % 4][:],
                                     xb4[t % 2][:, fc * P:(fc + 1) * P],
                                     ident[:])
                        sTP4.tag(te)

            do_barrier()

        with contextlib.ExitStack() as S5:
            e5 = S5.enter_context
            mcol = e5(nc.sbuf_tensor("mcol", [P, 32], F32))
            mcg = e5(nc.sbuf_tensor("mcg", [P, 32], F32))
            selE = e5(nc.sbuf_tensor("selE", [8, 512], F32))
            qcv = e5(nc.sbuf_tensor("qcv", [P, 1], F32))
            scv = e5(nc.sbuf_tensor("scv", [1, 16], F32))
            cT = e5(nc.sbuf_tensor("cT", [8, 1], F32))
            cvec = e5(nc.sbuf_tensor("cvec", [P, 4], F32))
            hsc = e5(nc.sbuf_tensor("hsc", [P, 4 * NT], BF16))

            sMR = Sem(e5(nc.semaphore("sMR")))
            sMW = Sem(e5(nc.semaphore("sMW")))
            sMWc = Sem(e5(nc.semaphore("sMWc")))
            sL5 = Sem(e5(nc.semaphore("sL5")))
            sL5g = Sem(e5(nc.semaphore("sL5g")))
            sHM = Sem(e5(nc.semaphore("sHM")))
            sHEv = Sem(e5(nc.semaphore("sHEv")))
            sCF = Sem(e5(nc.semaphore("sCF")))

            with contextlib.ExitStack() as S5a:
                e5a = S5a.enter_context
                ldsb = e5a(nc.sbuf_tensor("ldsb", [P, 32, 512], BF16))
                wqf = e5a(nc.sbuf_tensor("wqf", [P, 32, 128], F32))
                keyst = e5a(nc.sbuf_tensor("keyst", [P, 8], F32))
                hsb = e5a(nc.sbuf_tensor("hsb", [P, 4 * NT], F32))
                psq = e5a(nc.psum_tensor("psq", [P, 1], F32))
                pssc = e5a(nc.psum_tensor("pssc", [1, 8], F32))
                psct = e5a(nc.psum_tensor("psct", [8, 1], F32))
                pscv = e5a(nc.psum_tensor("pscv", [P, 1], F32))
                psh = [e5a(nc.psum_tensor(f"psh{i}", [P, 512], F32))
                       for i in (0, 1)]

                @sec(block.sync, "sy")
                def _(sy):
                    sy.dma_start(out=wqf[:].rearrange("p a b -> p (a b)"),
                                 in_=wqfT.rearrange("p a b -> p (a b)"))
                    sL5.tag(sy, 16)
                    sy.dma_start(out=keyst[:], in_=keysT)
                    sL5.tag(sy, 16)
                    sy.dma_start(out=selE[:], in_=selEc)
                    sL5.tag(sy, 16)

                @sec(block.gpsimd, "gp")
                def _(gp):
                    gp.dma_start(out=ldsb[:].rearrange("p a b -> p (a b)"),
                                 in_=ldh.rearrange("p a b -> p (a b)"))
                    sL5g.tag(gp, 16)
                    gp.wait_ge(sMR.h, 33)
                    gp.dma_start(out=mc_in, in_=mcol[:])
                    sMW.tag(gp, 16)
                    gp.wait_ge(sMW.h, 16)
                    gp.collective_compute(
                        "AllReduce", ALU.add, replica_groups=RG,
                        ins=[mc_in.opt()], outs=[mc_out.opt()])
                    sMWc.tag(gp, 1)
                    gp.wait_ge(sMWc.h, 1)
                    gp.dma_start(out=mcg[:], in_=mc_out)
                    sMW.tag(gp, 16)

                @sec(block.tensor, "te")
                def _(te):
                    te.wait_ge(sL5.h, 16 * 3)
                    te.wait_ge(sL5g.h, 16)
                    te.wait_ge(sTE4.h, 256)
                    for uc in range(4):
                        for tt in range(2):
                            i = uc * 2 + tt
                            if i >= 2:
                                te.wait_ge(sHEv.h, i - 1)
                            for fc in range(32):
                                te.matmul(
                                    psh[i % 2][:],
                                    ldsb[:, fc, uc * P:(uc + 1) * P],
                                    x1fm[:, fc * NT + tt * 512:
                                         fc * NT + (tt + 1) * 512],
                                    start=(fc == 0), stop=(fc == 31))
                            sHM.tag(te)
                    te.wait_ge(sMW.h, 32)
                    for fc in range(32):
                        te.matmul(psq[:], wqf[:, fc, :], mcg[:, fc:fc + 1],
                                  start=(fc == 0), stop=(fc == 31))
                    sCF.tag(te, to=1)
                    te.wait_ge(sCF.h, 2)
                    te.matmul(pssc[0:1, :], qcv[:], keyst[:],
                              start=True, stop=True)
                    sCF.tag(te, to=3)
                    te.wait_ge(sCF.h, 6)
                    te.matmul(psct[:], scv[0:1, 0:8], ccol[0:1, 4:5],
                              start=True, stop=True)
                    sCF.tag(te, to=7)
                    for uc in range(4):
                        te.wait_ge(sCF.h, 8 + 2 * uc)
                        te.matmul(pscv[:], selE[:, uc * P:(uc + 1) * P],
                                  cT[:], start=True, stop=True)
                        sCF.tag(te, to=9 + 2 * uc)

                @sec(block.vector, "ve")
                def _(ve):
                    ve.wait_ge(sTE4.h, 256)
                    for fc in range(32):
                        ve.reduce_sum(mcol[:, fc:fc + 1],
                                      x1fm[:, fc * NT:(fc + 1) * NT], AX.X)
                        sMR.tag(ve)
                    ve.tensor_scalar(mcol[:], mcol[:], 1.0 / S, None,
                                     ALU.mult)
                    sMR.tag(ve)
                    for i in range(8):
                        ve.wait_ge(sHM.h, i + 1)
                        uc, tt = i // 2, i % 2
                        ve.tensor_copy(hsb[:, uc * NT + tt * 512:
                                           uc * NT + (tt + 1) * 512],
                                       psh[i % 2][:])
                        sHEv.tag(ve)
                    ve.wait_ge(sCF.h, 1)
                    ve.tensor_copy(qcv[:], psq[:])
                    sCF.tag(ve, to=2)
                    ve.wait_ge(sCF.h, 3)
                    ve.tensor_copy(scv[0:1, 0:8], pssc[0:1, :])
                    ve.tensor_scalar(scv[0:1, 0:8], scv[0:1, 0:8], SSCL,
                                     None, ALU.mult)
                    ve.reduce_max(scv[0:1, 8:9], scv[0:1, 0:8], AX.X,
                                  negate=True)
                    sCF.tag(ve, to=4)
                    ve.wait_ge(sCF.h, 5)
                    ve.reduce_sum(scv[0:1, 9:10], scv[0:1, 0:8], AX.X)
                    ve.reciprocal(scv[0:1, 10:11], scv[0:1, 9:10])
                    ve.tensor_scalar(scv[0:1, 0:8], scv[0:1, 0:8],
                                     scv[0:1, 10:11], None, ALU.mult)
                    sCF.tag(ve, to=6)
                    ve.wait_ge(sCF.h, 7)
                    ve.tensor_copy(cT[:], psct[:])
                    sCF.tag(ve, to=8)
                    for uc in range(4):
                        ve.wait_ge(sCF.h, 9 + 2 * uc)
                        ve.tensor_copy(cvec[:, uc:uc + 1], pscv[:])
                        sCF.tag(ve, to=10 + 2 * uc)
                    ve.wait_ge(sHEv.h, 8)
                    for uc in range(4):
                        ve.tensor_scalar(hsc[:, uc * NT:(uc + 1) * NT],
                                         hsb[:, uc * NT:(uc + 1) * NT],
                                         cvec[:, uc:uc + 1], None, ALU.mult)
                    sCF.tag(ve, to=17)

                @sec(block.scalar, "sc")
                def _(sc):
                    sc.wait_ge(sCF.h, 4)
                    sc.activation(scv[0:1, 0:8], scv[0:1, 0:8], AF.Exp,
                                  bias=scv[0:1, 8:9])
                    sCF.tag(sc, to=5)

            CF_DONE = 17

            with contextlib.ExitStack() as S5b:
                e5b = S5b.enter_context
                wob = [e5b(nc.sbuf_tensor(f"wob{i}", [P, 32, 512], BF16))
                       for i in (0, 1)]
                usb = [e5b(nc.sbuf_tensor(f"usb{i}", [P, 4, 512], BF16))
                       for i in (0, 1)]
                x1re = [e5b(nc.sbuf_tensor(f"x1re{i}", [P, 512], F32))
                        for i in (0, 1)]
                x2st = [e5b(nc.sbuf_tensor(f"x2st{i}", [P, 512], F32))
                        for i in (0, 1)]
                psf = [e5b(nc.psum_tensor(f"psf{i}", [P, 512], F32))
                       for i in range(3)]

                sWB5 = [Sem(e5b(nc.semaphore(f"sWB5{i}"))) for i in (0, 1)]
                sXR5 = [Sem(e5b(nc.semaphore(f"sXR5{i}"))) for i in (0, 1)]
                sPM5 = Sem(e5b(nc.semaphore("sPM5")))
                sXE5 = Sem(e5b(nc.semaphore("sXE5")))
                sXW5 = [Sem(e5b(nc.semaphore(f"sXW5{i}"))) for i in (0, 1)]

                @sec(block.gpsimd, "gp")
                def _(gp):
                    gp.wait_ge(sCF.h, CF_DONE)
                    for n in range(8):
                        if n >= 2:
                            gp.wait_ge(sPM5.h, (n - 1) * 8)
                        gp.dma_start(
                            out=wob[n % 2][:],
                            in_=worgT.rearrange("(fc p) o -> p fc o", p=P)
                            [:, :, n * 512:(n + 1) * 512])
                        sWB5[n % 2].tag(gp, 16)
                        gp.dma_start(out=usb[n % 2][:],
                                     in_=uh[:, :, n * 512:(n + 1) * 512])
                        sWB5[n % 2].tag(gp, 16)

                @sec(block.sync, "sy")
                def _(sy):
                    sy.wait_ge(sCF.h, CF_DONE)
                    for n in range(8):
                        for t in range(8):
                            i = n * 8 + t
                            if i >= 2:
                                sy.wait_ge(sXE5.h, i - 1)
                            sy.dma_start(
                                out=x1re[i % 2][:],
                                in_=x1_d[t * P:(t + 1) * P,
                                         n * 512:(n + 1) * 512])
                            sXR5[i % 2].tag(sy, 16)

                @sec(block.tensor, "te")
                def _(te):
                    te.wait_ge(sCF.h, CF_DONE)
                    for n in range(8):
                        te.wait_ge(sWB5[n % 2].h, 32 * (n // 2 + 1))
                        for t in range(8):
                            i = n * 8 + t
                            if i >= 3:
                                te.wait_ge(sXE5.h, i - 2)
                            for fc in range(32):
                                te.matmul(psf[i % 3][:],
                                          x1fm[:, fc * NT + t * P:
                                               fc * NT + (t + 1) * P],
                                          wob[n % 2][:, fc, :],
                                          start=(fc == 0), stop=False)
                            for uc in range(4):
                                te.matmul(psf[i % 3][:],
                                          hsc[:, uc * NT + t * P:
                                              uc * NT + (t + 1) * P],
                                          usb[n % 2][:, uc, :],
                                          start=False, stop=(uc == 3))
                            sPM5.tag(te)

                @sec(block.vector, "ve")
                def _(ve):
                    for i in range(64):
                        n, t = i // 8, i % 8
                        ve.wait_ge(sPM5.h, i + 1)
                        ve.wait_ge(sXR5[i % 2].h, 16 * (i // 2 + 1))
                        if i >= 2:
                            ve.wait_ge(sXW5[i % 2].h, 16 * (i // 2))
                        ve.tensor_tensor(x2st[i % 2][:], psf[i % 3][:],
                                         x1re[i % 2][:], ALU.add)
                        sXE5.tag(ve)
                        ve.tensor_scalar(dmj[:], x2st[i % 2][:], 0.0, 0.0,
                                         ALU.add, ALU.add,
                                         accum_out=prtS[:, i % 2:i % 2 + 1])
                        ve.wait_ge(sSQ3.h, 64 + i + 1)
                        pS = prtS[:, i % 2:i % 2 + 1]
                        pQ = prtQ[:, i % 2:i % 2 + 1]
                        if n == 0:
                            ve.tensor_copy(stB[:, 16 + t:17 + t], pS)
                            ve.tensor_copy(stB[:, t:t + 1], pQ)
                        else:
                            ve.tensor_tensor(stB[:, 16 + t:17 + t],
                                             stB[:, 16 + t:17 + t], pS,
                                             ALU.add)
                            ve.tensor_tensor(stB[:, t:t + 1],
                                             stB[:, t:t + 1], pQ, ALU.add)
                        sST3.tag(ve)

                @sec(block.scalar, "sc")
                def _(sc):
                    for n in range(8):
                        for t in range(8):
                            i = n * 8 + t
                            sc.wait_ge(sXE5.h, i + 1)
                            if i >= 2:
                                sc.wait_ge(sST3.h, 64 + i - 1)
                            sc.activation(sqj[:], x2st[i % 2][:], AF.Square,
                                          accum_out=prtQ[:, i % 2:i % 2 + 1])
                            sSQ3.tag(sc)
                            sc.dma_start(
                                out=x2raw_d[t * P:(t + 1) * P,
                                            n * 512:(n + 1) * 512],
                                in_=x2st[i % 2][:])
                            sXW5[i % 2].tag(sc, 16)
                    sc.wait_ge(sXW5[0].h, 16 * 32)
                    sc.wait_ge(sXW5[1].h, 16 * 32)

                do_barrier()

        SP45.close()

        with contextlib.ExitStack() as S6:
            e6 = S6.enter_context
            xr6 = [e6(nc.sbuf_tensor(f"xr6{i}", [P, D], F32)) for i in (0, 1)]
            xo6 = [e6(nc.sbuf_tensor(f"xo6{i}", [P, D], F32)) for i in (0, 1)]
            sL6 = [Sem(e6(nc.semaphore(f"sL6{i}"))) for i in (0, 1)]
            sNM6 = Sem(e6(nc.semaphore("sNM6")))
            sW6 = [Sem(e6(nc.semaphore(f"sW6{i}"))) for i in (0, 1)]
            sST6 = Sem(e6(nc.semaphore("sST6")))

            @sec(block.sync, "sy")
            def _(sy):
                for t in range(8):
                    if t >= 2:
                        sy.wait_ge(sNM6.h, t - 1)
                    sy.dma_start(out=xr6[t % 2][:],
                                 in_=x2raw_d[t * P:(t + 1) * P, :])
                    sL6[t % 2].tag(sy, 16)

            @sec(block.vector, "ve")
            def _(ve):
                ve.tensor_scalar(stB[:, 8:16], stB[:, 16:24], 1.0 / D,
                                 None, ALU.mult)
                ve.tensor_scalar(stB[:, 0:8], stB[:, 0:8], 1.0 / D,
                                 None, ALU.mult)
                ve.tensor_tensor(stB[:, 24:32], stB[:, 8:16],
                                 stB[:, 8:16], ALU.mult)
                ve.scalar_tensor_tensor(stB[:, 0:8], stB[:, 0:8], 1e-5,
                                        stB[:, 24:32], ALU.add, ALU.subtract)
                sST6.tag(ve, to=1)
                ve.wait_ge(sST6.h, 2)
                ve.reciprocal(stB[:, 24:32], stB[:, 24:32])
                for t in range(8):
                    ve.wait_ge(sL6[t % 2].h, 16 * (t // 2 + 1))
                    if t >= 2:
                        ve.wait_ge(sW6[t % 2].h, 16 * ((t - 2) // 2 + 1))
                    ve.tensor_scalar(xo6[t % 2][:], xr6[t % 2][:],
                                     stB[:, 8 + t:9 + t],
                                     stB[:, 24 + t:25 + t],
                                     ALU.subtract, ALU.mult)
                    sNM6.tag(ve)

            @sec(block.scalar, "sc")
            def _(sc):
                sc.wait_ge(sST6.h, 1)
                sc.sqrt(stB[:, 24:32], stB[:, 0:8])
                sST6.tag(sc, to=2)
                for t in range(8):
                    sc.wait_ge(sNM6.h, t + 1)
                    sc.dma_start(out=out[t * P:(t + 1) * P, :],
                                 in_=xo6[t % 2][:])
                    sW6[t % 2].tag(sc, 16)
                sc.wait_ge(sW6[0].h, 16 * 4)
                sc.wait_ge(sW6[1].h, 16 * 4)

            do_barrier()

    return nc


def _host_inputs(inputs):
    X = np.ascontiguousarray(np.asarray(inputs["L0_attn_hidden_states"],
                                        np.float32).reshape(S, D))
    Wq = np.asarray(inputs["L0_attn_Wq"], np.float32)
    Wk = np.asarray(inputs["L0_attn_Wk"], np.float32)
    Wv = np.asarray(inputs["L0_attn_Wv"], np.float32)
    Wo = np.asarray(inputs["L0_attn_Wo"], np.float32)
    Worg = np.asarray(inputs["L0_ffn_W_org"], np.float32)
    ld = np.asarray(inputs["L0_ffn_lora_down"], np.float32)
    lu = np.asarray(inputs["L0_ffn_lora_up"], np.float32)
    lk = np.asarray(inputs["L0_ffn_lora_keys"], np.float32)
    Wqf = np.asarray(inputs["L0_ffn_Wq"], np.float32)

    worgT = np.ascontiguousarray(Worg.T)
    ldh = np.ascontiguousarray(
        ld.transpose(1, 0, 2).reshape(32, 128, 512).transpose(1, 0, 2))
    uh = np.ascontiguousarray(
        lu.reshape(512, D).reshape(4, 128, D).transpose(1, 0, 2))
    wqfT = np.ascontiguousarray(
        Wqf.T.reshape(32, 128, 128).transpose(1, 0, 2))
    keysT = np.ascontiguousarray(lk.T)

    tric = np.where(np.arange(P)[None, :] >= np.arange(P)[:, None],
                    0.0, MASKV).astype(np.float32)
    gidx = np.arange(128)
    mq_g = np.where(gidx < 64, 7.0, np.where(gidx < 96, 31.0, 127.0))
    mqvc = np.broadcast_to(mq_g.astype(np.float32), (P, P)).copy()
    imqvc = np.broadcast_to((1.0 / mq_g).astype(np.float32), (P, P)).copy()
    identc = np.eye(P, dtype=np.float32).astype(BF)
    selEc = np.zeros((8, 512), np.float32)
    for uc in range(4):
        for p in range(128):
            selEc[2 * uc + p // 64, uc * 128 + p] = 1.0

    inv = (1.0 / (10000.0 ** (np.arange(0, HD, 2, dtype=np.float64) / HD))
           ).astype(np.float32)

    maps = []
    for c in range(W):
        pos = np.arange(c * NT, (c + 1) * NT, dtype=np.float32)
        ang = (pos[None, :] * inv[:, None]).astype(np.float32)
        cc_half = np.cos(ang.astype(np.float64)).astype(np.float32)
        ss_half = np.sin(ang.astype(np.float64)).astype(np.float32)
        cct = np.concatenate([cc_half, cc_half], axis=0)
        sst = np.concatenate([ss_half, -ss_half], axis=0)
        mq_o = 7.0 if c < 4 else (31.0 if c < 6 else 127.0)
        ccolc = np.zeros((P, 8), np.float32)
        ccolc[:, 0] = SSCL
        ccolc[:, 1] = EXPC
        ccolc[:, 2] = mq_o
        ccolc[:, 3] = 1.0 / mq_o
        ccolc[:, 4] = 1.0
        ccolc[:, 5] = 0.0
        maps.append({
            "hid": np.ascontiguousarray(X[c * NT:(c + 1) * NT]),
            "wkS": np.ascontiguousarray(Wk[c * 512:(c + 1) * 512]),
            "wqS": np.ascontiguousarray(Wq[c * 512:(c + 1) * 512]),
            "wvS": np.ascontiguousarray(Wv[c * 512:(c + 1) * 512]),
            "woS": np.ascontiguousarray(Wo[c * 512:(c + 1) * 512]),
            "worgT": worgT,
            "ldh": ldh,
            "uh": uh,
            "wqfT": wqfT,
            "keysT": keysT,
            "cctab": np.ascontiguousarray(cct),
            "sstab": np.ascontiguousarray(sst),
            "tric": tric,
            "mqvc": mqvc,
            "imqvc": imqvc,
            "identc": identc,
            "selEc": selEc,
            "ccolc": ccolc,
        })
    return maps


def kernel(**inputs):
    maps = _host_inputs(inputs)
    with allow_ici_all_to_all():
        nc = build_nc()
    res = run_bass_kernel_spmd(nc, maps, core_ids=list(range(W)))
    outp = np.empty((S, D), np.float32)
    for c in range(W):
        outp[c * NT:(c + 1) * NT] = np.asarray(res.results[c]["out"],
                                               np.float32)
    return outp.reshape(1, S, D)
